# version 29
# baseline (speedup 1.0000x reference)
import numpy as np
import concourse.bass as bass
import concourse.mybir as mybir
from concourse.bass_utils import run_bass_kernel_spmd
from contextlib import ExitStack

F32 = mybir.dt.float32
BF16 = mybir.dt.bfloat16
AF = mybir.ActivationFunctionType
ALU = mybir.AluOpType
AX = mybir.AxisListType

D = 1024
NCORES = 8
SCALE = 0.03125


class Prog:

    ENG_SEMS = {"sync": "q_sy", "tensor": "q_t", "vector": "q_v",
                "scalar": "q_s", "gpsimd": "q_g"}

    def __init__(self):
        self.ops = []
        self.marks = {}
        self.counts = {}

    def add(self, eng, fn, waits=(), mark=None, dma_sem=None):
        rw = []
        for w in waits:
            if w is None:
                continue
            if w not in self.marks:
                raise KeyError(f"wait on unmarked key {w!r} (op mark={mark!r})")
            rw.append(self.marks[w])
        mi = None
        if mark is not None:
            assert mark not in self.marks, f"duplicate mark {mark!r}"
            sem = dma_sem if dma_sem is not None else self.ENG_SEMS[eng]
            inc = 16 if dma_sem is not None else 1
            v = self.counts.get(sem, 0) + inc
            self.counts[sem] = v
            self.marks[mark] = (sem, v)
            mi = (sem, v, inc)
        else:
            assert dma_sem is None
        self.ops.append((eng, fn, tuple(rw), mi))

    def emit(self, block, sems):
        waited = {e: {} for e in self.ENG_SEMS}

        def make_body(engname):
            def body(eng):
                seen = waited[engname]
                for e, fn, rw, mi in self.ops:
                    if e != engname:
                        continue
                    for semname, val in rw:
                        if seen.get(semname, -1) >= val:
                            continue
                        eng.wait_ge(sems[semname], val)
                        seen[semname] = val
                    ins = fn(eng)
                    if mi is not None:
                        semname, val, inc = mi
                        ins.then_inc(sems[semname], inc)
            return body

        for engname in ("sync", "gpsimd", "vector", "tensor", "scalar"):
            getattr(block, engname)(make_body(engname))


def build_nc(T=4096):
    NT = T // 128
    NTB = T // 512
    NGRP = NT // 4
    full = (T == 4096)

    nc = bass.Bass("TRN2", target_bir_lowering=False,
                   dynamic_dma_scratch_size=8192)

    x_d = nc.dram_tensor("x", [T, D], F32, kind="ExternalInput").ap()
    w1_d = nc.dram_tensor("W1", [D, D], F32, kind="ExternalInput").ap()
    b1_d = nc.dram_tensor("b1", [D], F32, kind="ExternalInput").ap()
    w2_d = nc.dram_tensor("W2", [D, D], F32, kind="ExternalInput").ap()
    b2_d = nc.dram_tensor("b2", [D], F32, kind="ExternalInput").ap()
    w3_d = nc.dram_tensor("W3", [D, D], F32, kind="ExternalInput").ap()
    b3_d = nc.dram_tensor("b3", [D], F32, kind="ExternalInput").ap()
    w4_d = nc.dram_tensor("W4", [D, D], F32, kind="ExternalInput").ap()
    b4_d = nc.dram_tensor("b4", [D], F32, kind="ExternalInput").ap()
    g1r_d = nc.dram_tensor("G1R", [1, 32, 4, 32], F32, kind="ExternalInput").ap()
    g2r_d = nc.dram_tensor("G2R", [4, 16, 1, 16], F32, kind="ExternalInput").ap()
    g1i_d = nc.dram_tensor("G1I", [1, 32, 4, 32], F32, kind="ExternalInput").ap()
    g2i_d = nc.dram_tensor("G2I", [4, 16, 1, 16], F32, kind="ExternalInput").ap()
    br_d = nc.dram_tensor("brl", [512], F32, kind="ExternalInput").ap()
    bi_d = nc.dram_tensor("bim", [512], F32, kind="ExternalInput").ap()
    out_d = nc.dram_tensor("out", [T, D], F32, kind="ExternalOutput").ap()
    scr_d = [nc.dram_tensor(f"scr{ri}", [1024, 256], F32).ap() for ri in range(2)]

    es = ExitStack()
    E = es.enter_context

    KT = E(nc.sbuf_tensor("KT", [128, 8, T], BF16))
    V0 = E(nc.sbuf_tensor("V0", [128, NT, D], BF16))
    Wq2 = E(nc.sbuf_tensor("Wq2", [128, 4, D], BF16))
    Wproj = E(nc.sbuf_tensor("Wproj", [128, 8, D], BF16))
    Ssb = E(nc.sbuf_tensor("Ssb", [128, T], F32))
    QXT = E(nc.sbuf_tensor("QXT", [128, 8192], BF16))
    ngxT = E(nc.sbuf_tensor("ngxT", [128, 4, 512], BF16))
    xg = E(nc.sbuf_tensor("xg", [128, 2, D], F32))
    xr = E(nc.sbuf_tensor("xr", [128, D], F32))
    xnn = E(nc.sbuf_tensor("xnn", [128, 2, D], BF16))
    Et = E(nc.sbuf_tensor("Et", [128, 4, 512], BF16))
    ETs = E(nc.sbuf_tensor("ETs", [128, 4, 4, 128], BF16))
    ident = E(nc.sbuf_tensor("ident", [128, 128], BF16))
    cbs = E(nc.sbuf_tensor("cbs", [2, 1152], BF16))
    b1s = E(nc.sbuf_tensor("b1s", [128, 8], F32))
    b2s = E(nc.sbuf_tensor("b2s", [128, 8], F32))
    b3c = E(nc.sbuf_tensor("b3c", [128, 8], BF16))
    bn6 = E(nc.sbuf_tensor("bn6", [128, 2, 12], F32))
    bn2 = E(nc.sbuf_tensor("bn2", [128, 2, 2], F32))
    sd1 = E(nc.sbuf_tensor("sd1", [128, 2, 1], F32))
    vep = E(nc.sbuf_tensor("vep", [128, 2, 1], F32))
    rstd = E(nc.sbuf_tensor("rstd", [128, 2, 1], F32))
    nmr = E(nc.sbuf_tensor("nmr", [128, 2, 1], F32))
    mx8 = E(nc.sbuf_tensor("mx8", [128, 8], F32))
    mxv = E(nc.sbuf_tensor("mxv", [128, 1], F32))
    ngm = E(nc.sbuf_tensor("ngm", [128, 1], F32))
    r8 = E(nc.sbuf_tensor("r8", [128, 8], F32))
    rsm = E(nc.sbuf_tensor("rsm", [128, 1], F32))
    rrv = E(nc.sbuf_tensor("rrv", [128, 1], F32))

    cbhi = cbs[0:1, 0:1024]
    cb2 = cbs[0:2, 0:1024]
    ones1 = cbs[0:2, 1024:1152]
    cblo = None
    QT = QXT[:, 0:4096].rearrange("p (c n) -> p c n", c=8)
    xnT = QXT[:, 4096:8192].rearrange("p (u c n) -> p u c n", u=4, c=8)
    if full:
        W3T = KT[:, :, 2048:3072]
    else:
        W3T = None

    if full:
        W3B = KT[:, :, 1024:2048]
        W4b = KT[:, :, 3072:4096]
        W3p = Ssb[:, :].bitcast(BF16).rearrange("p (c n) -> p c n", c=8)
        pro_flat = V0[:, :, :].rearrange("p a n -> p (a n)")
    else:
        W3B_t = E(nc.sbuf_tensor("W3B_t", [128, 8, D], BF16))
        W4b_t = E(nc.sbuf_tensor("W4b_t", [128, 8, D], BF16))
        W3p_t = E(nc.sbuf_tensor("W3p_t", [128, 8, D], BF16))
        W3T_t = E(nc.sbuf_tensor("W3T_t", [128, 8, D], BF16))
        pro_t = E(nc.sbuf_tensor("pro_t", [128, 19456], BF16))
        W3B = W3B_t[:, :, :]
        W4b = W4b_t[:, :, :]
        W3p = W3p_t[:, :, :]
        W3T = W3T_t[:, :, :]
        pro_flat = pro_t[:, :]

    out2 = pro_flat[:, 0:4096].bitcast(F32).rearrange("p (q n) -> p q n", q=8)
    wqst = [pro_flat[:, 4096 + 1024 * k:4096 + 1024 * (k + 1)].bitcast(F32)
            for k in range(2)]
    b4row = pro_flat[0:1, 6144:8192].bitcast(F32)
    cbrow = pro_flat[0:1, 8192:10240].bitcast(F32)
    cbf = pro_flat[0:1, 10240:12288].bitcast(F32)
    g1p = pro_flat[0:4, 12288:16384].bitcast(F32).rearrange(
        "p (ri n) -> p ri n", ri=2)
    g2n = pro_flat[0:4, 16384:17408].bitcast(F32).rearrange(
        "p (ri n) -> p ri n", ri=2)
    g2p = pro_flat[0:4, 17408:18432].bitcast(F32).rearrange(
        "p (ri n) -> p ri n", ri=2)
    lo_stage = pro_flat[0:1, 18432:19456]

    sp_a = E(nc.psum_tensor("sp_a", [128, 512], F32))
    sp_b = E(nc.psum_tensor("sp_b", [128, 512], F32))
    po = E(nc.psum_tensor("po", [128, 1024], F32))
    pf = E(nc.psum_tensor("pf", [128, 1024], F32))
    ptA = E(nc.psum_tensor("ptA", [128, 4, 128], BF16))
    ptB = E(nc.psum_tensor("ptB", [128, 4, 128], BF16))

    semnames = ["q_sy", "q_t", "q_v", "q_s", "q_g",
                "d_pro", "d_wg", "d_xg0", "d_xg1", "d_xr", "d_out",
                "d_scw", "d_sra", "d_srb", "d_w10", "d_w11", "d_w12", "d_w13", "d_cbl", "d_wgB", "d_wgC", "d_wgD"]
    sems = {n: E(nc.semaphore(n)) for n in semnames}

    P = Prog()
    free = {"ptA": None, "ptB": None, "po": None, "pf": None,
            "sp_a": None, "sp_b": None}

    def proload(key, out_ap, in_ap):
        P.add("sync", lambda e, o=out_ap, i=in_ap: e.dma_start(o, i),
              mark=key, dma_sem="d_pro")

    proload(("Lg1r",), g1p[:, 0, :],
            bass.AP(g1r_d.tensor, 0, [[32, 4], [128, 32], [1, 32]]))
    proload(("Lg1i",), g1p[:, 1, :],
            bass.AP(g1i_d.tensor, 0, [[32, 4], [128, 32], [1, 32]]))
    proload(("Lg2r",), g2n[:, 0, :],
            bass.AP(g2r_d.tensor, 0, [[256, 4], [1, 256]]))
    proload(("Lg2i",), g2n[:, 1, :],
            bass.AP(g2i_d.tensor, 0, [[256, 4], [1, 256]]))
    for c in range(8):
        proload(("Lb1", c), b1s[:, c:c + 1], b1_d[c * 128:(c + 1) * 128].unsqueeze(1))
        proload(("Lb2", c), b2s[:, c:c + 1], b2_d[c * 128:(c + 1) * 128].unsqueeze(1))
    proload(("Lb4",), b4row[:, :], b4_d.unsqueeze(0))
    proload(("Lbr",), cbrow[:, 0:512], br_d.unsqueeze(0))
    proload(("Lbi",), cbrow[:, 512:1024], bi_d.unsqueeze(0))
    PRO = ("Lbi",)

    P.add("gpsimd", lambda e: e.memset(ident[:, :], 0.0), mark=("id1",))
    P.add("gpsimd", lambda e: e.affine_select(
        out=ident[:, :], in_=ident[:, :], compare_op=ALU.not_equal, fill=1.0,
        base=0, pattern=[[-1, 128]], channel_multiplier=1),
        waits=[("id1",)], mark=("id2",))
    P.add("gpsimd", lambda e: e.memset(ones1, 1.0), mark=("ones",))
    P.add("gpsimd", lambda e: e.dma_start(
        Wproj[:, :, :], w2_d.rearrange("(c p) n -> p c n", p=128)),
        mark=("LW2",), dma_sem="d_wg")
    P.add("gpsimd", lambda e: e.dma_start(
        W3B, w3_d.rearrange("(c p) n -> p c n", p=128)),
        mark=("LW3",), dma_sem="d_wgC")
    P.add("gpsimd", lambda e: e.dma_start(
        W4b, w4_d.rearrange("(c p) n -> p c n", p=128)),
        mark=("LW4",), dma_sem="d_wgB")
    P.add("gpsimd", lambda e: e.dma_start(
        b3c[:, :], bass.AP(b3_d.tensor, 0, [[1, 128], [128, 8]]),
        allow_slow_non_contiguous=True), mark=("Lb3c",), dma_sem="d_wgD")

    W3PDONE = CBDONE = WQDONE = None

    def emit_pro_tail():
        nonlocal W3PDONE, CBDONE, WQDONE
        P.add("vector", lambda e: e.tensor_copy(
            g2p[:, :, :].rearrange("p ri (g e) -> p ri g e", g=16),
            g2n[:, :, :].rearrange("p ri (e g) -> p ri g e", e=16)),
            waits=[PRO], mark=("g2perm",))

        def emit_g(ri):
            for q in range(8):
                P.add("tensor", lambda e, ri=ri, q=q: e.matmul(
                    sp_a[:, 0:256], g1p[:, ri, q * 128:(q + 1) * 128],
                    g2p[:, ri, :], start=True, stop=True),
                    waits=[PRO, ("g2perm",), free["sp_a"],
                           ("Gev", ri, q - 1) if q > 0 else
                           (("Gev", ri - 1, 7) if ri > 0 else None)],
                    mark=("Gmm", ri, q))
                P.add("vector", lambda e, ri=ri, q=q: e.tensor_copy(
                    out2[:, q, :], sp_a[:, 0:256]),
                    waits=[("Gmm", ri, q)] + ([("scw", 0)] if ri == 1 else []),
                    mark=("Gev", ri, q))
            P.add("sync", lambda e, ri=ri: e.dma_start(
                scr_d[ri].rearrange("(q p) n -> p q n", p=128), out2[:, :, :]),
                waits=[("Gev", ri, 7)],
                mark=("scw", ri), dma_sem="d_scw")
            free["sp_a"] = ("Gev", ri, 7)

    
        for p2 in range(32):
            i, j2 = divmod(p2, 4)
            bank, bk = (ptA, "ptA") if p2 % 2 == 0 else (ptB, "ptB")
            j0 = 2 * j2
            P.add("tensor", lambda e, i=i, j0=j0, bank=bank: e.transpose(
                bank[:, 0, :], W3B[:, i, j0 * 128:(j0 + 1) * 128], ident[:, :]),
                waits=[("LW3",), ("id2",), free[bk]], mark=None)
            P.add("tensor", lambda e, i=i, j0=j0, bank=bank: e.transpose(
                bank[:, 1, :], W3B[:, i, (j0 + 1) * 128:(j0 + 2) * 128], ident[:, :]),
                mark=("w3tp", p2))
            P.add("vector", lambda e, i=i, j0=j0, bank=bank: e.tensor_copy(
                W3T[:, j0:j0 + 2, i * 128:(i + 1) * 128], bank[:, 0:2, :]),
                waits=[("w3tp", p2)], mark=("w3te", p2))
            free[bk] = ("w3te", p2)
        emit_g(0)
        emit_g(1)


        for ri in range(2):
            for q in range(4):
                use = ri * 4 + q
                slot = use % 2
                dsem = "d_sra" if slot == 0 else "d_srb"
                prev = ("Wqcv", (use - 2) // 4, (use - 2) % 4) if use >= 2 else None
                for dl in range(8):
                    P.add("sync", lambda e, ri=ri, q=q, dl=dl, slot=slot: e.dma_start(
                        wqst[slot][16 * dl:16 * dl + 16, :],
                        bass.AP(scr_d[ri].tensor, q * 2048 + dl * 256,
                                [[16, 16], [8192, 32], [1, 16]])),
                        waits=[("scw", ri), prev], mark=("rb", ri, q, dl),
                        dma_sem=dsem)
                P.add("vector", lambda e, ri=ri, q=q, slot=slot: e.tensor_copy(
                    Wq2[:, q, ri * 512:(ri + 1) * 512], wqst[slot][:, :]),
                    waits=[("rb", ri, q, 7)], mark=("Wqcv", ri, q))
        WQDONE = ("Wqcv", 1, 3)


        for m in range(8):
            for h in range(2):
                for k in range(8):
                    P.add("tensor", lambda e, m=m, h=h, k=k: e.matmul(
                        po[:, h * 512:(h + 1) * 512], W3T[:, k, m * 128:(m + 1) * 128],
                        W4b[:, k, h * 512:(h + 1) * 512],
                        start=(k == 0), stop=(k == 7)),
                        waits=[("w3te", 31), ("LW4",),
                               ("W3pe", m - 1) if m > 0 else None],
                        mark=("W3pm", m, h) if k == 7 else None)
            P.add("scalar", lambda e, m=m: e.activation(
                W3p[:, m, :], po[:, :], AF.Identity),
                waits=[("W3pm", m, 0), ("W3pm", m, 1)], mark=("W3pe", m))
        free["po"] = ("W3pe", 7)
        W3PDONE = ("W3pe", 7)


        for h in range(2):
            for k in range(8):
                P.add("tensor", lambda e, h=h, k=k: e.matmul(
                    pf[0:1, h * 512:(h + 1) * 512], b3c[:, k:k + 1],
                    W4b[:, k, h * 512:(h + 1) * 512], start=(k == 0), stop=(k == 7)),
                    waits=[("LW4",), ("Lb3c",), free["pf"]],
                    mark=("cbm", h) if k == 7 else None)
        P.add("vector", lambda e: e.tensor_copy(cbf[:, :], pf[0:1, :]),
              waits=[("cbm", 0), ("cbm", 1)], mark=("cb1",))
        P.add("vector", lambda e: e.tensor_add(cbf[:, :], cbf[:, :], b4row[:, :]),
              waits=[("cb1",), PRO], mark=("cb2",))
        P.add("vector", lambda e: e.tensor_add(cbf[:, :], cbf[:, :], cbrow[:, :]),
              waits=[("cb2",)], mark=("cb3",))
        P.add("vector", lambda e: e.tensor_copy(cbhi, cbf[:, :]),
              waits=[("cb3",)], mark=("cb4",))
        P.add("vector", lambda e: e.tensor_tensor(cbf[:, :], cbf[:, :], cbhi, ALU.subtract),
              waits=[("cb4",)], mark=("cb5",))
        P.add("vector", lambda e: e.tensor_copy(lo_stage, cbf[:, :]),
              waits=[("cb5",)], mark=("cb6",))
        P.add("sync", lambda e: e.dma_start(cbs[1:2, 0:1024], lo_stage),
              waits=[("cb6",)], mark=("cb7",), dma_sem="d_cbl")
        CBDONE = ("cb7",)
        free["pf"] = ("cb1",)


    def emit_ln_load(gt):
        slot = gt % 2
        i = gt % NT
        dsem = "d_xg0" if slot == 0 else "d_xg1"
        pw = [("lnap", gt - 2), ("bns", gt - 2, 1)] if gt >= 2 else []
        P.add("scalar", lambda e, i=i, slot=slot: e.dma_start(
            xg[:, slot, :], x_d[i * 128:(i + 1) * 128, :]),
            waits=pw, mark=("xg", gt), dma_sem=dsem)

    def emit_ln_stats(gt):
        s = gt % 2
        P.add("vector", lambda e, s=s: e.bn_stats(
            bn6[:, s, 0:6], xg[:, s, 0:512]),
            waits=[("xg", gt)], mark=("bns", gt, 0))
        P.add("vector", lambda e, s=s: e.bn_stats(
            bn6[:, s, 6:12], xg[:, s, 512:1024]),
            waits=[("bns", gt, 0)], mark=("bns", gt, 1))
        P.add("vector", lambda e, s=s: e.bn_aggr(bn2[:, s, :], bn6[:, s, :]),
              waits=[("bns", gt, 1),
                     ("sqr", gt - 2) if gt >= 2 else None],
              mark=("bna", gt))
        P.add("vector", lambda e, s=s: e.tensor_scalar_add(
            vep[:, s, :], bn2[:, s, 1:2], 1e-5),
            waits=[("bna", gt), ("sqr", gt - 2) if gt >= 2 else None],
            mark=("vep", gt))

    def emit_ln_apply(gt):
        s = gt % 2
        P.add("scalar", lambda e, s=s: e.activation(
            sd1[:, s, :], vep[:, s, :], AF.Ln),
            waits=[("vep", gt), ("rstd", gt - 2) if gt >= 2 else None],
            mark=("sqr", gt))
        P.add("scalar", lambda e, s=s: e.activation(
            rstd[:, s, :], sd1[:, s, :], AF.Exp, scale=-0.5),
            waits=[("sqr", gt), ("lnap", gt - 2) if gt >= 2 else None],
            mark=("rstd", gt))
        P.add("vector", lambda e, s=s: e.tensor_scalar(
            nmr[:, s, :], bn2[:, s, 0:1], rstd[:, s, 0:1], -1.0,
            ALU.mult, ALU.mult),
            waits=[("rstd", gt)], mark=("nmr", gt))
        P.add("gpsimd", lambda e, s=s: e.tensor_scalar(
            xnn[:, s, :], xg[:, s, :], rstd[:, s, 0:1], nmr[:, s, 0:1],
            ALU.mult, ALU.add),
            waits=[("nmr", gt), ("xtp", gt - 2, 1) if gt >= 2 else None],
            mark=("lnap", gt))

    def emit_ln_chain(gt):
        emit_ln_load(gt)
        emit_ln_stats(gt)
        emit_ln_apply(gt)

    def emit_ln_tps(gt, phase, extra_tp):
        u = gt % 4
        slot = gt % 2
        for q2 in range(2):
            bank, bk = (ptA, "ptA") if q2 == 0 else (ptB, "ptB")
            c0 = 4 * q2
            xw = [("lnap", gt), free[bk]] + (list(extra_tp) if q2 == 0 else [])
            for qq in range(4):
                P.add("tensor", lambda e, c0=c0, qq=qq, slot=slot, bank=bank:
                      e.transpose(bank[:, qq, :],
                                  xnn[:, slot, (c0 + qq) * 128:(c0 + qq + 1) * 128],
                                  ident[:, :]),
                      waits=(xw + [("id2",)]) if qq == 0 else [],
                      mark=("xtp", gt, q2) if qq == 3 else None)
            P.add("vector", lambda e, u=u, c0=c0, bank=bank: e.tensor_copy(
                xnT[:, u, c0:c0 + 4, :], bank[:, :, :]),
                waits=[("xtp", gt, q2)], mark=("xte", gt, q2))
            free[bk] = ("xte", gt, q2)
        if phase == 2:
            P.add("vector", lambda e, u=u: e.tensor_scalar_mul(
                ngxT[:, u, :],
                xnT[:, u, 4:8, :].rearrange("p c n -> p (c n)"), -1.0),
                waits=[("xte", gt, 1)], mark=("ngx", gt))

    def emit_ln_tile(gt, phase, extra_tp):
        emit_ln_chain(gt)
        emit_ln_tps(gt, phase, extra_tp)

    def emit_khalf(g, half):
        off = 256 * half
        for c in range(8):
            spv, spk = (sp_a, "sp_a") if c % 2 == 0 else (sp_b, "sp_b")
            for k in range(8):
                P.add("tensor", lambda e, c=c, k=k, half=half, spv=spv, off=off:
                      e.matmul(spv[:, off:off + 256],
                               Wproj[:, k, c * 128:(c + 1) * 128],
                               xnT[:, 2 * half:2 * half + 2, k, :],
                               start=(k == 0), stop=(k == 7)),
                      waits=[("xte", 4 * g + 2 * half + 1, 1), ("LW2",),
                             free[spk]],
                      mark=("Kmm", half, g, c) if k == 7 else None)
            kw = [("Kmm", half, g, c), PRO]
            if full and g in (2, 3):
                kw.append(("w3tp", 31))
            elif full and g in (4, 5):
                kw.append(("W3pe", 7))
            elif full and g in (6, 7):
                kw += [("W3pe", 7), ("cbm", 1)]
            P.add("scalar", lambda e, c=c, g=g, spv=spv, off=off: e.activation(
                KT[:, c, g * 512 + off:g * 512 + off + 256],
                spv[:, off:off + 256], AF.Identity,
                bias=b2s[:, c:c + 1]), waits=kw, mark=("Kev", half, g, c))
            free[spk] = ("Kev", half, g, c)

    def emit_v(g, u):
        pv, pk = (po, "po") if u % 2 == 0 else (pf, "pf")
        for h in range(2):
            for k in range(8):
                P.add("tensor", lambda e, u=u, h=h, k=k, pv=pv: e.matmul(
                    pv[:, h * 512:(h + 1) * 512], xnT[:, u, k, :],
                    W3p[:, k, h * 512:(h + 1) * 512],
                    start=(k == 0), stop=(k == 7)),
                    waits=[("xte", 4 * g + u, 1), W3PDONE, free[pk]],
                    mark=("Vmm", g, u, h) if k == 7 else None)
        vw = [("Vmm", g, u, 0), ("Vmm", g, u, 1)]
        if full and (4 * g + u) < 19:
            vw += [WQDONE, CBDONE]
        P.add("scalar", lambda e, g=g, u=u, pv=pv: e.activation(
            V0[:, 4 * g + u, :], pv[:, :], AF.Identity),
            waits=vw, mark=("Vev", g, u))
        free[pk] = ("Vev", g, u)

    for u in range(2):
        emit_ln_load(u)
        emit_ln_stats(u)
    for u in range(2):
        emit_ln_apply(u)
    for u in range(2):
        emit_ln_tps(u, 1, [])
    for u in range(2, 4):
        emit_ln_load(u)
        emit_ln_stats(u)
    for u in range(2, 4):
        emit_ln_apply(u)
    for u in range(2, 4):
        emit_ln_tps(u, 1, [])
    for g in range(NGRP):
        nphase = 2 if g == NGRP - 1 else 1
        for u2 in range(2):
            emit_ln_load(4 * (g + 1) + u2)
            emit_ln_stats(4 * (g + 1) + u2)
        emit_khalf(g, 0)
        for u2 in range(2):
            emit_ln_apply(4 * (g + 1) + u2)
        if g == 0:
            emit_khalf(g, 1)
            emit_pro_tail()
        emit_v(g, 0)
        emit_v(g, 1)
        for u2 in range(2):
            emit_ln_tps(4 * (g + 1) + u2, nphase,
                        [("Kmm", 0, g, 7), ("Vmm", g, u2, 1)])
        for u2 in range(2, 4):
            emit_ln_load(4 * (g + 1) + u2)
            emit_ln_stats(4 * (g + 1) + u2)
        if g > 0:
            emit_khalf(g, 1)
        for u2 in range(2, 4):
            emit_ln_apply(4 * (g + 1) + u2)
        emit_v(g, 2)
        emit_v(g, 3)
        for u2 in range(2, 4):
            emit_ln_tps(4 * (g + 1) + u2, nphase,
                        [("Kmm", 1, g, 7), ("Vmm", g, u2, 1)])

    KTDONE = ("Kev", 1, NGRP - 1, 7)
    V0DONE = ("Vev", NGRP - 1, 3)

    w1B = w1_d.rearrange("(k p) (r n) -> p r k n", p=128, n=128)
    w1stage = Ssb[:, :].rearrange("p (s n) -> p s n", s=4)
    for r in range(8):
        lw = [("Kmm", 1, NGRP - 1, 7), V0DONE]
        if r >= 4:
            lw.append(("LW1", r - 4))
        P.add("sync", lambda e, r=r: e.dma_start(
            w1stage[:, r % 4, :], w1B[:, r, :, :]),
            waits=lw, mark=("LW1d", r), dma_sem="d_w1" + str(r % 4))
        P.add("vector", lambda e, r=r: e.tensor_copy(
            Wproj[:, r, :], w1stage[:, r % 4, :]),
            waits=[("LW1d", r)], mark=("LW1", r))

    def emit_q2(g):
        gt3 = NT + 4 * g + 3
        for r in range(8):
            spv, spk = (sp_a, "sp_a") if r % 2 == 0 else (sp_b, "sp_b")
            for k in range(8):
                P.add("tensor", lambda e, r=r, k=k, spv=spv: e.matmul(
                    spv[:, :], Wproj[:, r, k * 128:(k + 1) * 128],
                    xnT[:, :, k, :], start=(k == 0), stop=(k == 7)),
                    waits=[("xte", gt3, 1), ("LW1", r), free[spk]],
                    mark=("Qmm", g, r) if k == 7 else None)
            qevw = [("Qmm", g, r), PRO]
            if g >= 1:
                qevw.append(("Smm", 4 * g - 1, NTB - 1))
            P.add("scalar", lambda e, r=r, spv=spv: e.activation(
                QT[:, r, :], spv[:, :], AF.Identity, bias=b1s[:, r:r + 1]),
                waits=qevw, mark=("Qev", g, r))
            free[spk] = ("Qev", g, r)

    emit_q2(0)
    for g in range(NGRP):
        QDONE = ("Qev", g, 7)
        for si in range(4):
            i = 4 * g + si
            gt = NT + i
            P.add("sync", lambda e, i=i: e.dma_start(
                xr[:, :], x_d[i * 128:(i + 1) * 128, :]),
                waits=[("out", i - 1)] if i >= 1 else [], mark=("xrd", i),
                dma_sem="d_xr")
            if g + 1 < NGRP:
                emit_ln_load(NT + 4 * (g + 1) + si)
            free["sp_c"] = free["po"]
            free["sp_d"] = free["po"]
            sslots = [(sp_a[:, :], "sp_a"), (sp_b[:, :], "sp_b"),
                      (po[:, 0:512], "sp_c"), (po[:, 512:1024], "sp_d")]
            for tb in range(NTB):
                spv, spk = sslots[tb % 4]
                sw = [QDONE, KTDONE, free[spk]]
                for c in range(8):
                    P.add("tensor", lambda e, tb=tb, c=c, si=si, spv=spv: e.matmul(
                        spv, QT[:, c, si * 128:(si + 1) * 128],
                        KT[:, c, tb * 512:(tb + 1) * 512],
                        start=(c == 0), stop=(c == 7)),
                        waits=sw, mark=("Smm", i, tb) if c == 7 else None)
                scw_ = [("Smm", i, tb)]
                if i >= 1:
                    scw_.append(("exp", i - 1, tb))
                else:
                    scw_.append(("LW1", 7))
                P.add("scalar", lambda e, tb=tb, spv=spv: e.activation(
                    Ssb[:, tb * 512:(tb + 1) * 512], spv, AF.Identity),
                    waits=scw_, mark=("Scp", i, tb))
                P.add("vector", lambda e, tb=tb, spv=spv: e.reduce_max(
                    mx8[:, tb:tb + 1], spv, axis=AX.X),
                    waits=[("Smm", i, tb), ("Scp", i, tb)],
                    mark=("Smx", i, tb))
                free[spk] = ("Smx", i, tb)
            P.add("vector", lambda e: e.reduce_max(
                mxv[:, :], mx8[:, 0:NTB], axis=AX.X),
                waits=[("Smx", i, NTB - 1)], mark=("mxc", i))
            P.add("vector", lambda e: e.tensor_scalar_mul(
                ngm[:, :], mxv[:, :], -SCALE),
                waits=[("mxc", i)] +
                      ([("exp", i - 1, NTB - 1)] if i >= 1 else []),
                mark=("ngm", i))
            for o in range(2):
                fw = [("xte", gt, 1), WQDONE, CBDONE, ("ones",), free["pf"],
                      ("ngx", gt)]
                reg = pf[:, o * 512:(o + 1) * 512]
                for c in range(4):
                    P.add("tensor", lambda e, o=o, c=c, si=si, reg=reg: e.matmul(
                        reg, xnT[:, si, c, :], Wq2[:, c, o * 512:(o + 1) * 512],
                        start=(c == 0), stop=False), waits=fw, mark=None)
                for cc in range(4):
                    if o == 0:
                        P.add("tensor", lambda e, cc=cc, si=si, reg=reg: e.matmul(
                            reg, ngxT[:, si, cc * 128:(cc + 1) * 128],
                            Wq2[:, cc, 512:1024], start=False, stop=False),
                            mark=None)
                    else:
                        P.add("tensor", lambda e, cc=cc, si=si, reg=reg: e.matmul(
                            reg, xnT[:, si, 4 + cc, :], Wq2[:, cc, 0:512],
                            start=False, stop=False), mark=None)
                P.add("tensor", lambda e, o=o, reg=reg: e.matmul(
                    reg, ones1, cb2[:, o * 512:(o + 1) * 512],
                    start=False, stop=True), mark=("Fmm", i, o))
            def emit_exp(tb):
                ew = [("ngm", i), ("Scp", i, tb)]
                if tb >= 4:
                    pu = (i, tb - 4)
                elif i >= 1:
                    pu = (i - 1, NTB - 4 + tb) if NTB >= 4 else (i - 1, 0)
                else:
                    pu = None
                if pu is not None and pu[1] >= 0:
                    ew += [("r8",) + pu, ("etp",) + pu + (1,)]
                P.add("scalar", lambda e, tb=tb: e.activation(
                    Et[:, tb % 4, :], Ssb[:, tb * 512:(tb + 1) * 512],
                    AF.Exp, bias=ngm[:, 0:1], scale=SCALE),
                    waits=ew, mark=("exp", i, tb))
                P.add("vector", lambda e, tb=tb: e.reduce_sum(
                    r8[:, tb:tb + 1], Et[:, tb % 4, :], axis=AX.X),
                    waits=[("exp", i, tb)] +
                          ([("rr", i - 1)] if (tb == 0 and i >= 1) else []),
                    mark=("r8", i, tb))

            def emit_tp(tb):
                bank, bk = (ptA, "ptA") if tb % 2 == 0 else (ptB, "ptB")
                for q in range(4):
                    P.add("tensor", lambda e, tb=tb, q=q, bank=bank:
                          e.transpose(bank[:, q, :],
                                      Et[:, tb % 4, q * 128:(q + 1) * 128],
                                      ident[:, :]),
                          waits=[("exp", i, tb), free[bk]] if q == 0 else [],
                          mark=("etp", i, tb, 1) if q == 3 else None)
                if tb >= 4:
                    slotprev = ("Omm", i, tb - 4, 1)
                elif i >= 1:
                    slotprev = ("Omm", i - 1,
                                NTB - 4 + tb if NTB >= 4 else 0, 1)
                else:
                    slotprev = None
                P.add("vector", lambda e, tb=tb, bank=bank: e.tensor_copy(
                    ETs[:, tb % 4, :, :], bank[:, :, :]),
                    waits=[("etp", i, tb, 1), slotprev],
                    mark=("ete", i, tb, 1))
                free[bk] = ("ete", i, tb, 1)

            def emit_omm(tb):
                ow = [("ete", i, tb, 1), V0DONE]
                if tb == 0:
                    ow += [free["po"], ("cmb1", i - 1) if i >= 1 else None]
                    if NTB >= 3:
                        ow += [("Scp", i, NTB - 2), ("Scp", i, NTB - 1)]
                    else:
                        ow += [("Scp", i, tbq) for tbq in range(NTB)
                               if tbq % 4 >= 2]
                for q in range(4):
                    tj = 4 * tb + q
                    P.add("tensor", lambda e, tb=tb, q=q, tj=tj: e.matmul(
                        po[:, 0:512], ETs[:, tb % 4, q, :], V0[:, tj, 0:512],
                        start=(tj == 0), stop=(tj == NT - 1)),
                        waits=ow if q == 0 else [], mark=None)
                    P.add("tensor", lambda e, tb=tb, q=q, tj=tj: e.matmul(
                        po[:, 512:1024], ETs[:, tb % 4, q, :], V0[:, tj, 512:1024],
                        start=(tj == 0), stop=(tj == NT - 1)),
                        mark=("Omm", i, tb, 1) if q == 3 else None)

            emit_exp(0)
            emit_tp(0)
            for tb in range(1, NTB):
                emit_exp(tb)
                emit_tp(tb)
                emit_omm(tb - 1)
                if g + 1 < NGRP:
                    if tb == 1:
                        emit_ln_stats(NT + 4 * (g + 1) + si)
                    elif tb == 3:
                        emit_ln_apply(NT + 4 * (g + 1) + si)
                    elif tb == 5:
                        emit_ln_tps(NT + 4 * (g + 1) + si, 2, [("Fmm", i, 1)])
            emit_omm(NTB - 1)
            if g + 1 < NGRP and NTB < 6:
                if NTB < 4:
                    emit_ln_stats(NT + 4 * (g + 1) + si)
                    emit_ln_apply(NT + 4 * (g + 1) + si)
                emit_ln_tps(NT + 4 * (g + 1) + si, 2, [("Fmm", i, 1)])
            P.add("vector", lambda e: e.reduce_sum(
                rsm[:, :], r8[:, 0:NTB], axis=AX.X),
                waits=[("r8", i, NTB - 1)], mark=("rs", i))
            P.add("vector", lambda e: e.reciprocal(rrv[:, :], rsm[:, :]),
                  waits=[("rs", i)], mark=("rr", i))
            P.add("vector", lambda e: e.scalar_tensor_tensor(
                xr[:, :], po[:, :], rrv[:, 0:1], xr[:, :], ALU.mult, ALU.add),
                waits=[("Omm", i, NTB - 1, 1), ("rr", i), ("xrd", i)],
                mark=("cmb1", i))
            free["po"] = ("cmb1", i)
            P.add("vector", lambda e: e.tensor_add(xr[:, :], xr[:, :], pf[:, :]),
                  waits=[("Fmm", i, 0), ("Fmm", i, 1), ("cmb1", i)],
                  mark=("cmb2", i))
            free["pf"] = ("cmb2", i)
            P.add("sync", lambda e, i=i: e.dma_start(
                out_d[i * 128:(i + 1) * 128, :], xr[:, :]),
                waits=[("cmb2", i)], mark=("out", i), dma_sem="d_out")

        if g + 1 < NGRP:
            emit_q2(g + 1)

    P.add("sync", lambda e: e.nop(), waits=[("out", NT - 1)])

    with nc.Block() as block:
        P.emit(block, sems)

    es.close()
    nc.finalize()
    return nc


def kernel(L0_attn_x, L0_attn_W1, L0_attn_b1, L0_attn_W2, L0_attn_b2,
           L0_attn_W3, L0_attn_b3, L0_attn_W4, L0_attn_b4,
           L0_ffn_G1_R, L0_ffn_G2_R, L0_ffn_G1_I, L0_ffn_G2_I,
           L0_ffn_bias_real, L0_ffn_bias_imag):
    x = np.ascontiguousarray(np.asarray(L0_attn_x, dtype=np.float32))
    B, T, Dm = x.shape
    assert B == NCORES and Dm == D
    c32 = lambda a: np.ascontiguousarray(np.asarray(a, dtype=np.float32))
    shared = {
        "W1": c32(L0_attn_W1), "b1": c32(L0_attn_b1),
        "W2": c32(L0_attn_W2), "b2": c32(L0_attn_b2),
        "W3": c32(L0_attn_W3), "b3": c32(L0_attn_b3),
        "W4": c32(L0_attn_W4), "b4": c32(L0_attn_b4),
        "G1R": c32(L0_ffn_G1_R), "G2R": c32(L0_ffn_G2_R),
        "G1I": c32(L0_ffn_G1_I), "G2I": c32(L0_ffn_G2_I),
        "brl": c32(L0_ffn_bias_real), "bim": c32(L0_ffn_bias_imag),
    }
    nc = build_nc(T)
    in_maps = [dict(shared, x=np.ascontiguousarray(x[b])) for b in range(B)]
    res = run_bass_kernel_spmd(nc, in_maps, core_ids=list(range(NCORES)))
    out = np.stack([res.results[b]["out"] for b in range(B)], axis=0)
    return out.astype(np.float32)


# version 30 (final)
# speedup vs baseline: 1.2029x; 1.2029x over previous
import numpy as np
import concourse.bass as bass
import concourse.mybir as mybir
from concourse.bass_utils import run_bass_kernel_spmd
from contextlib import ExitStack

F32 = mybir.dt.float32
BF16 = mybir.dt.bfloat16
AF = mybir.ActivationFunctionType
ALU = mybir.AluOpType
AX = mybir.AxisListType

D = 1024
NCORES = 8
SCALE = 0.03125


class Prog:

    ENG_SEMS = {"sync": "q_sy", "tensor": "q_t", "vector": "q_v",
                "scalar": "q_s", "gpsimd": "q_g"}

    def __init__(self):
        self.ops = []
        self.marks = {}
        self.counts = {}

    def add(self, eng, fn, waits=(), mark=None, dma_sem=None):
        rw = []
        for w in waits:
            if w is None:
                continue
            if w not in self.marks:
                raise KeyError(f"wait on unmarked key {w!r} (op mark={mark!r})")
            rw.append(self.marks[w])
        mi = None
        if mark is not None:
            assert mark not in self.marks, f"duplicate mark {mark!r}"
            sem = dma_sem if dma_sem is not None else self.ENG_SEMS[eng]
            inc = 16 if dma_sem is not None else 1
            v = self.counts.get(sem, 0) + inc
            self.counts[sem] = v
            self.marks[mark] = (sem, v)
            mi = (sem, v, inc)
        else:
            assert dma_sem is None
        self.ops.append((eng, fn, tuple(rw), mi))

    def emit(self, block, sems):
        waited = {e: {} for e in self.ENG_SEMS}

        def make_body(engname):
            def body(eng):
                seen = waited[engname]
                for e, fn, rw, mi in self.ops:
                    if e != engname:
                        continue
                    for semname, val in rw:
                        if seen.get(semname, -1) >= val:
                            continue
                        eng.wait_ge(sems[semname], val)
                        seen[semname] = val
                    ins = fn(eng)
                    if mi is not None:
                        semname, val, inc = mi
                        ins.then_inc(sems[semname], inc)
            return body

        for engname in ("sync", "gpsimd", "vector", "tensor", "scalar"):
            getattr(block, engname)(make_body(engname))


def build_nc(T=4096):
    NT = T // 128
    NTB = T // 512
    NGRP = NT // 4
    full = (T == 4096)

    nc = bass.Bass("TRN2", target_bir_lowering=False,
                   dynamic_dma_scratch_size=8192)

    x_d = nc.dram_tensor("x", [T, D], F32, kind="ExternalInput").ap()
    w1_d = nc.dram_tensor("W1", [D, D], F32, kind="ExternalInput").ap()
    b1_d = nc.dram_tensor("b1", [D], F32, kind="ExternalInput").ap()
    w2_d = nc.dram_tensor("W2", [D, D], F32, kind="ExternalInput").ap()
    b2_d = nc.dram_tensor("b2", [D], F32, kind="ExternalInput").ap()
    w3_d = nc.dram_tensor("W3", [D, D], F32, kind="ExternalInput").ap()
    b3_d = nc.dram_tensor("b3", [D], F32, kind="ExternalInput").ap()
    w4_d = nc.dram_tensor("W4", [D, D], F32, kind="ExternalInput").ap()
    b4_d = nc.dram_tensor("b4", [D], F32, kind="ExternalInput").ap()
    g1r_d = nc.dram_tensor("G1R", [1, 32, 4, 32], F32, kind="ExternalInput").ap()
    g2r_d = nc.dram_tensor("G2R", [4, 16, 1, 16], F32, kind="ExternalInput").ap()
    g1i_d = nc.dram_tensor("G1I", [1, 32, 4, 32], F32, kind="ExternalInput").ap()
    g2i_d = nc.dram_tensor("G2I", [4, 16, 1, 16], F32, kind="ExternalInput").ap()
    br_d = nc.dram_tensor("brl", [512], F32, kind="ExternalInput").ap()
    bi_d = nc.dram_tensor("bim", [512], F32, kind="ExternalInput").ap()
    out_d = nc.dram_tensor("out", [T, D], F32, kind="ExternalOutput").ap()
    scr_d = [nc.dram_tensor(f"scr{ri}", [1024, 256], F32).ap() for ri in range(2)]

    es = ExitStack()
    E = es.enter_context

    KT = E(nc.sbuf_tensor("KT", [128, 8, T], BF16))
    V0 = E(nc.sbuf_tensor("V0", [128, NT, D], BF16))
    Wq2 = E(nc.sbuf_tensor("Wq2", [128, 4, D], BF16))
    Wproj = E(nc.sbuf_tensor("Wproj", [128, 8, D], BF16))
    Ssb = E(nc.sbuf_tensor("Ssb", [128, T], F32))
    QXT = E(nc.sbuf_tensor("QXT", [128, 8192], BF16))
    ngxT = E(nc.sbuf_tensor("ngxT", [128, 4, 512], BF16))
    xg = E(nc.sbuf_tensor("xg", [128, 2, D], F32))
    xr = E(nc.sbuf_tensor("xr", [128, D], F32))
    xnn = E(nc.sbuf_tensor("xnn", [128, 2, D], BF16))
    Et = E(nc.sbuf_tensor("Et", [128, 4, 512], BF16))
    ETs = E(nc.sbuf_tensor("ETs", [128, 4, 4, 128], BF16))
    ident = E(nc.sbuf_tensor("ident", [128, 128], BF16))
    cbs = E(nc.sbuf_tensor("cbs", [2, 1152], BF16))
    b1s = E(nc.sbuf_tensor("b1s", [128, 8], F32))
    b2s = E(nc.sbuf_tensor("b2s", [128, 8], F32))
    b3c = E(nc.sbuf_tensor("b3c", [128, 8], BF16))
    bn6 = E(nc.sbuf_tensor("bn6", [128, 2, 12], F32))
    bn2 = E(nc.sbuf_tensor("bn2", [128, 2, 2], F32))
    sd1 = E(nc.sbuf_tensor("sd1", [128, 2, 1], F32))
    vep = E(nc.sbuf_tensor("vep", [128, 2, 1], F32))
    rstd = E(nc.sbuf_tensor("rstd", [128, 2, 1], F32))
    nmr = E(nc.sbuf_tensor("nmr", [128, 2, 1], F32))
    mx8 = E(nc.sbuf_tensor("mx8", [128, 8], F32))
    mxv = E(nc.sbuf_tensor("mxv", [128, 1], F32))
    ngm = E(nc.sbuf_tensor("ngm", [128, 1], F32))
    r8 = E(nc.sbuf_tensor("r8", [128, 8], F32))
    rsm = E(nc.sbuf_tensor("rsm", [128, 1], F32))
    rrv = E(nc.sbuf_tensor("rrv", [128, 1], F32))

    cbhi = cbs[0:1, 0:1024]
    cb2 = cbs[0:2, 0:1024]
    ones1 = cbs[0:2, 1024:1152]
    cblo = None
    QT = QXT[:, 0:4096].rearrange("p (c n) -> p c n", c=8)
    xnT = QXT[:, 4096:8192].rearrange("p (u c n) -> p u c n", u=4, c=8)
    if full:
        W3T = KT[:, :, 2048:3072]
    else:
        W3T = None

    if full:
        W3B = KT[:, :, 1024:2048]
        W4b = KT[:, :, 3072:4096]
        W3p = Ssb[:, :].bitcast(BF16).rearrange("p (c n) -> p c n", c=8)
        pro_flat = V0[:, :, :].rearrange("p a n -> p (a n)")
    else:
        W3B_t = E(nc.sbuf_tensor("W3B_t", [128, 8, D], BF16))
        W4b_t = E(nc.sbuf_tensor("W4b_t", [128, 8, D], BF16))
        W3p_t = E(nc.sbuf_tensor("W3p_t", [128, 8, D], BF16))
        W3T_t = E(nc.sbuf_tensor("W3T_t", [128, 8, D], BF16))
        pro_t = E(nc.sbuf_tensor("pro_t", [128, 19456], BF16))
        W3B = W3B_t[:, :, :]
        W4b = W4b_t[:, :, :]
        W3p = W3p_t[:, :, :]
        W3T = W3T_t[:, :, :]
        pro_flat = pro_t[:, :]

    out2 = pro_flat[:, 0:4096].bitcast(F32).rearrange("p (q n) -> p q n", q=8)
    wqst = [pro_flat[:, 4096 + 1024 * k:4096 + 1024 * (k + 1)].bitcast(F32)
            for k in range(2)]
    b4row = pro_flat[0:1, 6144:8192].bitcast(F32)
    cbrow = pro_flat[0:1, 8192:10240].bitcast(F32)
    cbf = pro_flat[0:1, 10240:12288].bitcast(F32)
    g1p = pro_flat[0:4, 12288:16384].bitcast(F32).rearrange(
        "p (ri n) -> p ri n", ri=2)
    g2n = pro_flat[0:4, 16384:17408].bitcast(F32).rearrange(
        "p (ri n) -> p ri n", ri=2)
    g2p = pro_flat[0:4, 17408:18432].bitcast(F32).rearrange(
        "p (ri n) -> p ri n", ri=2)
    lo_stage = pro_flat[0:1, 18432:19456]

    sp_a = E(nc.psum_tensor("sp_a", [128, 512], F32))
    sp_b = E(nc.psum_tensor("sp_b", [128, 512], F32))
    po = E(nc.psum_tensor("po", [128, 1024], F32))
    pf = E(nc.psum_tensor("pf", [128, 1024], F32))
    ptA = E(nc.psum_tensor("ptA", [128, 4, 128], BF16))
    ptB = E(nc.psum_tensor("ptB", [128, 4, 128], BF16))

    semnames = ["q_sy", "q_t", "q_v", "q_s", "q_g",
                "d_pro", "d_wg", "d_xg0", "d_xg1", "d_xr", "d_out",
                "d_scw", "d_sra", "d_srb", "d_w10", "d_w11", "d_w12", "d_w13", "d_cbl", "d_wgB", "d_wgC", "d_wgD"]
    sems = {n: E(nc.semaphore(n)) for n in semnames}

    P = Prog()
    free = {"ptA": None, "ptB": None, "po": None, "pf": None,
            "sp_a": None, "sp_b": None}

    def proload(key, out_ap, in_ap):
        P.add("sync", lambda e, o=out_ap, i=in_ap: e.dma_start(o, i),
              mark=key, dma_sem="d_pro")

    proload(("Lg1r",), g1p[:, 0, :],
            bass.AP(g1r_d.tensor, 0, [[32, 4], [128, 32], [1, 32]]))
    proload(("Lg1i",), g1p[:, 1, :],
            bass.AP(g1i_d.tensor, 0, [[32, 4], [128, 32], [1, 32]]))
    proload(("Lg2r",), g2n[:, 0, :],
            bass.AP(g2r_d.tensor, 0, [[256, 4], [1, 256]]))
    proload(("Lg2i",), g2n[:, 1, :],
            bass.AP(g2i_d.tensor, 0, [[256, 4], [1, 256]]))
    for c in range(8):
        proload(("Lb1", c), b1s[:, c:c + 1], b1_d[c * 128:(c + 1) * 128].unsqueeze(1))
        proload(("Lb2", c), b2s[:, c:c + 1], b2_d[c * 128:(c + 1) * 128].unsqueeze(1))
    proload(("Lb4",), b4row[:, :], b4_d.unsqueeze(0))
    proload(("Lbr",), cbrow[:, 0:512], br_d.unsqueeze(0))
    proload(("Lbi",), cbrow[:, 512:1024], bi_d.unsqueeze(0))
    PRO = ("Lbi",)

    P.add("gpsimd", lambda e: e.memset(ident[:, :], 0.0), mark=("id1",))
    P.add("gpsimd", lambda e: e.affine_select(
        out=ident[:, :], in_=ident[:, :], compare_op=ALU.not_equal, fill=1.0,
        base=0, pattern=[[-1, 128]], channel_multiplier=1),
        waits=[("id1",)], mark=("id2",))
    P.add("gpsimd", lambda e: e.memset(ones1, 1.0), mark=("ones",))
    P.add("gpsimd", lambda e: e.dma_start(
        Wproj[:, :, :], w2_d.rearrange("(c p) n -> p c n", p=128)),
        mark=("LW2",), dma_sem="d_wg")
    P.add("gpsimd", lambda e: e.dma_start(
        W3B, w3_d.rearrange("(c p) n -> p c n", p=128)),
        mark=("LW3",), dma_sem="d_wgC")
    P.add("gpsimd", lambda e: e.dma_start(
        W4b, w4_d.rearrange("(c p) n -> p c n", p=128)),
        mark=("LW4",), dma_sem="d_wgB")
    P.add("gpsimd", lambda e: e.dma_start(
        b3c[:, :], bass.AP(b3_d.tensor, 0, [[1, 128], [128, 8]]),
        allow_slow_non_contiguous=True), mark=("Lb3c",), dma_sem="d_wgD")

    W3PDONE = CBDONE = WQDONE = None

    gregs = [(sp_a, 0, "sp_a"), (sp_b, 0, "sp_b"), (po, 0, "po"), (po, 512, "po")]

    def emit_g(ri):
        for q in range(8):
            spv, off, _ = gregs[q % 4]
            P.add("tensor", lambda e, ri=ri, q=q, spv=spv, off=off: e.matmul(
                spv[:, off:off + 256], g1p[:, ri, q * 128:(q + 1) * 128],
                g2p[:, ri, :], start=True, stop=True),
                waits=[PRO, ("g2perm",),
                       ("Gev", ri, q - 4) if q >= 4 else
                       (("Gev", ri - 1, q + 4) if ri > 0 else None)],
                mark=("Gmm", ri, q))
            P.add("vector", lambda e, q=q, spv=spv, off=off: e.tensor_copy(
                out2[:, q, :], spv[:, off:off + 256]),
                waits=[("Gmm", ri, q)] + ([("scw", 0)] if ri == 1 else []),
                mark=("Gev", ri, q))
        P.add("sync", lambda e, ri=ri: e.dma_start(
            scr_d[ri].rearrange("(q p) n -> p q n", p=128), out2[:, :, :]),
            waits=[("Gev", ri, 7)],
            mark=("scw", ri), dma_sem="d_scw")

    def emit_pro_tail():
        nonlocal W3PDONE, CBDONE, WQDONE
        for p4 in range(16):
            i, j4 = divmod(p4, 2)
            bank, bk = (ptA, "ptA") if p4 % 2 == 0 else (ptB, "ptB")
            j0 = 4 * j4
            for qq in range(4):
                P.add("tensor", lambda e, i=i, j0=j0, qq=qq, bank=bank:
                      e.transpose(bank[:, qq, :],
                                  W3B[:, i, (j0 + qq) * 128:(j0 + qq + 1) * 128],
                                  ident[:, :]),
                      waits=[("LW3",), ("id2",), free[bk]] if qq == 0 else [],
                      mark=("w3tp", p4) if qq == 3 else None)
            P.add("vector", lambda e, i=i, j0=j0, bank=bank: e.tensor_copy(
                W3T[:, j0:j0 + 4, i * 128:(i + 1) * 128], bank[:, :, :]),
                waits=[("w3tp", p4)], mark=("w3te", p4))
            free[bk] = ("w3te", p4)


        for ri in range(2):
            for q in range(4):
                use = ri * 4 + q
                slot = use % 2
                dsem = "d_sra" if slot == 0 else "d_srb"
                prev = ("Wqcv", (use - 2) // 4, (use - 2) % 4) if use >= 2 else None
                for dl in range(8):
                    P.add("sync", lambda e, ri=ri, q=q, dl=dl, slot=slot: e.dma_start(
                        wqst[slot][16 * dl:16 * dl + 16, :],
                        bass.AP(scr_d[ri].tensor, q * 2048 + dl * 256,
                                [[16, 16], [8192, 32], [1, 16]])),
                        waits=[("scw", ri), prev], mark=("rb", ri, q, dl),
                        dma_sem=dsem)
                P.add("vector", lambda e, ri=ri, q=q, slot=slot: e.tensor_copy(
                    Wq2[:, q, ri * 512:(ri + 1) * 512], wqst[slot][:, :]),
                    waits=[("rb", ri, q, 7)], mark=("Wqcv", ri, q))
        WQDONE = ("Wqcv", 1, 3)


        for m in range(8):
            for h in range(2):
                for k in range(8):
                    P.add("tensor", lambda e, m=m, h=h, k=k: e.matmul(
                        po[:, h * 512:(h + 1) * 512], W3T[:, k, m * 128:(m + 1) * 128],
                        W4b[:, k, h * 512:(h + 1) * 512],
                        start=(k == 0), stop=(k == 7)),
                        waits=[("w3te", 15), ("LW4",),
                               ("W3pe", m - 1) if m > 0 else
                               ("Gev", 1, 6 + h)],
                        mark=("W3pm", m, h) if k == 7 else None)
            P.add("scalar", lambda e, m=m: e.activation(
                W3p[:, m, :], po[:, :], AF.Identity),
                waits=[("W3pm", m, 0), ("W3pm", m, 1)], mark=("W3pe", m))
        free["po"] = ("W3pe", 7)
        W3PDONE = ("W3pe", 7)


        for h in range(2):
            for k in range(8):
                P.add("tensor", lambda e, h=h, k=k: e.matmul(
                    pf[0:1, h * 512:(h + 1) * 512], b3c[:, k:k + 1],
                    W4b[:, k, h * 512:(h + 1) * 512], start=(k == 0), stop=(k == 7)),
                    waits=[("LW4",), ("Lb3c",), free["pf"]],
                    mark=("cbm", h) if k == 7 else None)
        P.add("vector", lambda e: e.tensor_copy(cbf[:, :], pf[0:1, :]),
              waits=[("cbm", 0), ("cbm", 1)], mark=("cb1",))
        P.add("vector", lambda e: e.tensor_add(cbf[:, :], cbf[:, :], b4row[:, :]),
              waits=[("cb1",), PRO], mark=("cb2",))
        P.add("vector", lambda e: e.tensor_add(cbf[:, :], cbf[:, :], cbrow[:, :]),
              waits=[("cb2",)], mark=("cb3",))
        P.add("vector", lambda e: e.tensor_copy(cbhi, cbf[:, :]),
              waits=[("cb3",)], mark=("cb4",))
        P.add("vector", lambda e: e.tensor_tensor(cbf[:, :], cbf[:, :], cbhi, ALU.subtract),
              waits=[("cb4",)], mark=("cb5",))
        P.add("vector", lambda e: e.tensor_copy(lo_stage, cbf[:, :]),
              waits=[("cb5",)], mark=("cb6",))
        P.add("sync", lambda e: e.dma_start(cbs[1:2, 0:1024], lo_stage),
              waits=[("cb6",)], mark=("cb7",), dma_sem="d_cbl")
        CBDONE = ("cb7",)
        free["pf"] = ("cb1",)


    def emit_ln_load(gt):
        slot = gt % 2
        i = gt % NT
        dsem = "d_xg0" if slot == 0 else "d_xg1"
        pw = [("lnap", gt - 2), ("bns", gt - 2, 1)] if gt >= 2 else []
        P.add("scalar", lambda e, i=i, slot=slot: e.dma_start(
            xg[:, slot, :], x_d[i * 128:(i + 1) * 128, :]),
            waits=pw, mark=("xg", gt), dma_sem=dsem)

    def emit_ln_stats(gt):
        s = gt % 2
        P.add("vector", lambda e, s=s: e.bn_stats(
            bn6[:, s, 0:6], xg[:, s, 0:512]),
            waits=[("xg", gt)], mark=("bns", gt, 0))
        P.add("vector", lambda e, s=s: e.bn_stats(
            bn6[:, s, 6:12], xg[:, s, 512:1024]),
            waits=[("bns", gt, 0)], mark=("bns", gt, 1))
        P.add("vector", lambda e, s=s: e.bn_aggr(bn2[:, s, :], bn6[:, s, :]),
              waits=[("bns", gt, 1),
                     ("sqr", gt - 2) if gt >= 2 else None],
              mark=("bna", gt))
        P.add("vector", lambda e, s=s: e.tensor_scalar_add(
            vep[:, s, :], bn2[:, s, 1:2], 1e-5),
            waits=[("bna", gt), ("sqr", gt - 2) if gt >= 2 else None],
            mark=("vep", gt))

    def emit_ln_apply(gt):
        s = gt % 2
        P.add("scalar", lambda e, s=s: e.activation(
            sd1[:, s, :], vep[:, s, :], AF.Ln),
            waits=[("vep", gt), ("rstd", gt - 2) if gt >= 2 else None],
            mark=("sqr", gt))
        P.add("scalar", lambda e, s=s: e.activation(
            rstd[:, s, :], sd1[:, s, :], AF.Exp, scale=-0.5),
            waits=[("sqr", gt), ("lnap", gt - 2) if gt >= 2 else None],
            mark=("rstd", gt))
        P.add("vector", lambda e, s=s: e.tensor_scalar(
            nmr[:, s, :], bn2[:, s, 0:1], rstd[:, s, 0:1], -1.0,
            ALU.mult, ALU.mult),
            waits=[("rstd", gt)], mark=("nmr", gt))
        P.add("gpsimd", lambda e, s=s: e.tensor_scalar(
            xnn[:, s, :], xg[:, s, :], rstd[:, s, 0:1], nmr[:, s, 0:1],
            ALU.mult, ALU.add),
            waits=[("nmr", gt), ("xtp", gt - 2, 1) if gt >= 2 else None],
            mark=("lnap", gt))

    def emit_ln_chain(gt):
        emit_ln_load(gt)
        emit_ln_stats(gt)
        emit_ln_apply(gt)

    def emit_ln_tps(gt, phase, extra_tp):
        u = gt % 4
        slot = gt % 2
        for q2 in range(2):
            bank, bk = (ptA, "ptA") if q2 == 0 else (ptB, "ptB")
            c0 = 4 * q2
            xw = [("lnap", gt), free[bk]] + (list(extra_tp) if q2 == 0 else [])
            for qq in range(4):
                P.add("tensor", lambda e, c0=c0, qq=qq, slot=slot, bank=bank:
                      e.transpose(bank[:, qq, :],
                                  xnn[:, slot, (c0 + qq) * 128:(c0 + qq + 1) * 128],
                                  ident[:, :]),
                      waits=(xw + [("id2",)]) if qq == 0 else [],
                      mark=("xtp", gt, q2) if qq == 3 else None)
            P.add("vector", lambda e, u=u, c0=c0, bank=bank: e.tensor_copy(
                xnT[:, u, c0:c0 + 4, :], bank[:, :, :]),
                waits=[("xtp", gt, q2)], mark=("xte", gt, q2))
            free[bk] = ("xte", gt, q2)
        if phase == 2:
            P.add("vector", lambda e, u=u: e.tensor_scalar_mul(
                ngxT[:, u, :],
                xnT[:, u, 4:8, :].rearrange("p c n -> p (c n)"), -1.0),
                waits=[("xte", gt, 1)], mark=("ngx", gt))

    def emit_ln_tile(gt, phase, extra_tp):
        emit_ln_chain(gt)
        emit_ln_tps(gt, phase, extra_tp)

    def emit_khalf(g, half):
        off = 256 * half
        for c in range(8):
            spv, spk = (sp_a, "sp_a") if c % 2 == 0 else (sp_b, "sp_b")
            for k in range(8):
                P.add("tensor", lambda e, c=c, k=k, half=half, spv=spv, off=off:
                      e.matmul(spv[:, off:off + 256],
                               Wproj[:, k, c * 128:(c + 1) * 128],
                               xnT[:, 2 * half:2 * half + 2, k, :],
                               start=(k == 0), stop=(k == 7)),
                      waits=[("xte", 4 * g + 2 * half + 1, 1), ("LW2",),
                             free[spk]],
                      mark=("Kmm", half, g, c) if k == 7 else None)
            kw = [("Kmm", half, g, c), PRO]
            if full and g in (2, 3):
                kw.append(("w3tp", 15))
            elif full and g in (4, 5):
                kw.append(("W3pe", 7))
            elif full and g in (6, 7):
                kw += [("W3pe", 7), ("cbm", 1)]
            P.add("scalar", lambda e, c=c, g=g, spv=spv, off=off: e.activation(
                KT[:, c, g * 512 + off:g * 512 + off + 256],
                spv[:, off:off + 256], AF.Identity,
                bias=b2s[:, c:c + 1]), waits=kw, mark=("Kev", half, g, c))
            free[spk] = ("Kev", half, g, c)

    def emit_v(g, u):
        pv, pk = (po, "po") if u % 2 == 0 else (pf, "pf")
        for h in range(2):
            for k in range(8):
                P.add("tensor", lambda e, u=u, h=h, k=k, pv=pv: e.matmul(
                    pv[:, h * 512:(h + 1) * 512], xnT[:, u, k, :],
                    W3p[:, k, h * 512:(h + 1) * 512],
                    start=(k == 0), stop=(k == 7)),
                    waits=[("xte", 4 * g + u, 1), W3PDONE, free[pk]],
                    mark=("Vmm", g, u, h) if k == 7 else None)
        vw = [("Vmm", g, u, 0), ("Vmm", g, u, 1)]
        if full and (4 * g + u) < 19:
            vw += [WQDONE, CBDONE]
        P.add("scalar", lambda e, g=g, u=u, pv=pv: e.activation(
            V0[:, 4 * g + u, :], pv[:, :], AF.Identity),
            waits=vw, mark=("Vev", g, u))
        free[pk] = ("Vev", g, u)

    P.add("vector", lambda e: e.tensor_copy(
        g2p[:, :, :].rearrange("p ri (g e) -> p ri g e", g=16),
        g2n[:, :, :].rearrange("p ri (e g) -> p ri g e", e=16)),
        waits=[PRO], mark=("g2perm",))
    emit_g(0)
    for u in range(2):
        emit_ln_load(u)
        emit_ln_stats(u)
    emit_g(1)
    free["sp_a"] = ("Gev", 1, 4)
    free["sp_b"] = ("Gev", 1, 5)
    for u in range(2):
        emit_ln_apply(u)
    for u in range(2):
        emit_ln_tps(u, 1, [])
    for u in range(2, 4):
        emit_ln_load(u)
        emit_ln_stats(u)
    for u in range(2, 4):
        emit_ln_apply(u)
    for u in range(2, 4):
        emit_ln_tps(u, 1, [])
    for g in range(NGRP):
        nphase = 2 if g == NGRP - 1 else 1
        for u2 in range(2):
            emit_ln_load(4 * (g + 1) + u2)
            emit_ln_stats(4 * (g + 1) + u2)
        emit_khalf(g, 0)
        for u2 in range(2):
            emit_ln_apply(4 * (g + 1) + u2)
        if g == 0:
            emit_khalf(g, 1)
            emit_pro_tail()
        emit_v(g, 0)
        emit_v(g, 1)
        for u2 in range(2):
            emit_ln_tps(4 * (g + 1) + u2, nphase,
                        [("Kmm", 0, g, 7), ("Vmm", g, u2, 1)])
        for u2 in range(2, 4):
            emit_ln_load(4 * (g + 1) + u2)
            emit_ln_stats(4 * (g + 1) + u2)
        if g > 0:
            emit_khalf(g, 1)
        for u2 in range(2, 4):
            emit_ln_apply(4 * (g + 1) + u2)
        emit_v(g, 2)
        emit_v(g, 3)
        for u2 in range(2, 4):
            emit_ln_tps(4 * (g + 1) + u2, nphase,
                        [("Kmm", 1, g, 7), ("Vmm", g, u2, 1)])

    KTDONE = ("Kev", 1, NGRP - 1, 7)
    V0DONE = ("Vev", NGRP - 1, 3)

    w1B = w1_d.rearrange("(k p) (r n) -> p r k n", p=128, n=128)
    w1stage = Ssb[:, :].rearrange("p (s n) -> p s n", s=4)
    for r in range(8):
        lw = [("Kmm", 1, NGRP - 1, 7), V0DONE]
        if r >= 4:
            lw.append(("LW1", r - 4))
        P.add("sync", lambda e, r=r: e.dma_start(
            w1stage[:, r % 4, :], w1B[:, r, :, :]),
            waits=lw, mark=("LW1d", r), dma_sem="d_w1" + str(r % 4))
        P.add("vector", lambda e, r=r: e.tensor_copy(
            Wproj[:, r, :], w1stage[:, r % 4, :]),
            waits=[("LW1d", r)], mark=("LW1", r))

    def emit_q2(g):
        gt3 = NT + 4 * g + 3
        for r in range(8):
            spv, spk = (sp_a, "sp_a") if r % 2 == 0 else (sp_b, "sp_b")
            for k in range(8):
                P.add("tensor", lambda e, r=r, k=k, spv=spv: e.matmul(
                    spv[:, :], Wproj[:, r, k * 128:(k + 1) * 128],
                    xnT[:, :, k, :], start=(k == 0), stop=(k == 7)),
                    waits=[("xte", gt3, 1), ("LW1", r), free[spk]],
                    mark=("Qmm", g, r) if k == 7 else None)
            qevw = [("Qmm", g, r), PRO]
            if g >= 1:
                qevw.append(("Smm", 4 * g - 1, NTB - 1))
            P.add("scalar", lambda e, r=r, spv=spv: e.activation(
                QT[:, r, :], spv[:, :], AF.Identity, bias=b1s[:, r:r + 1]),
                waits=qevw, mark=("Qev", g, r))
            free[spk] = ("Qev", g, r)

    emit_q2(0)
    for g in range(NGRP):
        QDONE = ("Qev", g, 7)
        for si in range(4):
            i = 4 * g + si
            gt = NT + i
            P.add("sync", lambda e, i=i: e.dma_start(
                xr[:, :], x_d[i * 128:(i + 1) * 128, :]),
                waits=[("out", i - 1)] if i >= 1 else [], mark=("xrd", i),
                dma_sem="d_xr")
            if g + 1 < NGRP:
                emit_ln_load(NT + 4 * (g + 1) + si)
            free["sp_c"] = free["po"]
            free["sp_d"] = free["po"]
            sslots = [(sp_a[:, :], "sp_a"), (sp_b[:, :], "sp_b"),
                      (po[:, 0:512], "sp_c"), (po[:, 512:1024], "sp_d")]
            for tb in range(NTB):
                spv, spk = sslots[tb % 4]
                sw = [QDONE, KTDONE, free[spk]]
                for c in range(8):
                    P.add("tensor", lambda e, tb=tb, c=c, si=si, spv=spv: e.matmul(
                        spv, QT[:, c, si * 128:(si + 1) * 128],
                        KT[:, c, tb * 512:(tb + 1) * 512],
                        start=(c == 0), stop=(c == 7)),
                        waits=sw, mark=("Smm", i, tb) if c == 7 else None)
                scw_ = [("Smm", i, tb)]
                if i >= 1:
                    scw_.append(("exp", i - 1, tb))
                else:
                    scw_.append(("LW1", 7))
                P.add("scalar", lambda e, tb=tb, spv=spv: e.activation(
                    Ssb[:, tb * 512:(tb + 1) * 512], spv, AF.Identity),
                    waits=scw_, mark=("Scp", i, tb))
                P.add("vector", lambda e, tb=tb, spv=spv: e.reduce_max(
                    mx8[:, tb:tb + 1], spv, axis=AX.X),
                    waits=[("Smm", i, tb), ("Scp", i, tb)],
                    mark=("Smx", i, tb))
                free[spk] = ("Smx", i, tb)
            P.add("vector", lambda e: e.reduce_max(
                mxv[:, :], mx8[:, 0:NTB], axis=AX.X),
                waits=[("Smx", i, NTB - 1)], mark=("mxc", i))
            P.add("vector", lambda e: e.tensor_scalar_mul(
                ngm[:, :], mxv[:, :], -SCALE),
                waits=[("mxc", i)] +
                      ([("exp", i - 1, NTB - 1)] if i >= 1 else []),
                mark=("ngm", i))
            for o in range(2):
                fw = [("xte", gt, 1), WQDONE, CBDONE, ("ones",), free["pf"],
                      ("ngx", gt)]
                reg = pf[:, o * 512:(o + 1) * 512]
                for c in range(4):
                    P.add("tensor", lambda e, o=o, c=c, si=si, reg=reg: e.matmul(
                        reg, xnT[:, si, c, :], Wq2[:, c, o * 512:(o + 1) * 512],
                        start=(c == 0), stop=False), waits=fw, mark=None)
                for cc in range(4):
                    if o == 0:
                        P.add("tensor", lambda e, cc=cc, si=si, reg=reg: e.matmul(
                            reg, ngxT[:, si, cc * 128:(cc + 1) * 128],
                            Wq2[:, cc, 512:1024], start=False, stop=False),
                            mark=None)
                    else:
                        P.add("tensor", lambda e, cc=cc, si=si, reg=reg: e.matmul(
                            reg, xnT[:, si, 4 + cc, :], Wq2[:, cc, 0:512],
                            start=False, stop=False), mark=None)
                P.add("tensor", lambda e, o=o, reg=reg: e.matmul(
                    reg, ones1, cb2[:, o * 512:(o + 1) * 512],
                    start=False, stop=True), mark=("Fmm", i, o))
            def emit_exp(tb):
                ew = [("ngm", i), ("Scp", i, tb)]
                if tb >= 4:
                    pu = (i, tb - 4)
                elif i >= 1:
                    pu = (i - 1, NTB - 4 + tb) if NTB >= 4 else (i - 1, 0)
                else:
                    pu = None
                if pu is not None and pu[1] >= 0:
                    ew += [("r8",) + pu, ("etp",) + pu + (1,)]
                P.add("scalar", lambda e, tb=tb: e.activation(
                    Et[:, tb % 4, :], Ssb[:, tb * 512:(tb + 1) * 512],
                    AF.Exp, bias=ngm[:, 0:1], scale=SCALE),
                    waits=ew, mark=("exp", i, tb))
                P.add("vector", lambda e, tb=tb: e.reduce_sum(
                    r8[:, tb:tb + 1], Et[:, tb % 4, :], axis=AX.X),
                    waits=[("exp", i, tb)] +
                          ([("rr", i - 1)] if (tb == 0 and i >= 1) else []),
                    mark=("r8", i, tb))

            def emit_tp(tb):
                bank, bk = (ptA, "ptA") if tb % 2 == 0 else (ptB, "ptB")
                for q in range(4):
                    P.add("tensor", lambda e, tb=tb, q=q, bank=bank:
                          e.transpose(bank[:, q, :],
                                      Et[:, tb % 4, q * 128:(q + 1) * 128],
                                      ident[:, :]),
                          waits=[("exp", i, tb), free[bk]] if q == 0 else [],
                          mark=("etp", i, tb, 1) if q == 3 else None)
                if tb >= 4:
                    slotprev = ("Omm", i, tb - 4, 1)
                elif i >= 1:
                    slotprev = ("Omm", i - 1,
                                NTB - 4 + tb if NTB >= 4 else 0, 1)
                else:
                    slotprev = None
                P.add("vector", lambda e, tb=tb, bank=bank: e.tensor_copy(
                    ETs[:, tb % 4, :, :], bank[:, :, :]),
                    waits=[("etp", i, tb, 1), slotprev],
                    mark=("ete", i, tb, 1))
                free[bk] = ("ete", i, tb, 1)

            def emit_omm(tb):
                ow = [("ete", i, tb, 1), V0DONE]
                if tb == 0:
                    ow += [free["po"], ("cmb1", i - 1) if i >= 1 else None]
                    if NTB >= 3:
                        ow += [("Scp", i, NTB - 2), ("Scp", i, NTB - 1)]
                    else:
                        ow += [("Scp", i, tbq) for tbq in range(NTB)
                               if tbq % 4 >= 2]
                for q in range(4):
                    tj = 4 * tb + q
                    P.add("tensor", lambda e, tb=tb, q=q, tj=tj: e.matmul(
                        po[:, 0:512], ETs[:, tb % 4, q, :], V0[:, tj, 0:512],
                        start=(tj == 0), stop=(tj == NT - 1)),
                        waits=ow if q == 0 else [], mark=None)
                    P.add("tensor", lambda e, tb=tb, q=q, tj=tj: e.matmul(
                        po[:, 512:1024], ETs[:, tb % 4, q, :], V0[:, tj, 512:1024],
                        start=(tj == 0), stop=(tj == NT - 1)),
                        mark=("Omm", i, tb, 1) if q == 3 else None)

            emit_exp(0)
            emit_tp(0)
            for tb in range(1, NTB):
                emit_exp(tb)
                emit_tp(tb)
                emit_omm(tb - 1)
                if g + 1 < NGRP:
                    if tb == 1:
                        emit_ln_stats(NT + 4 * (g + 1) + si)
                    elif tb == 3:
                        emit_ln_apply(NT + 4 * (g + 1) + si)
                    elif tb == 5:
                        emit_ln_tps(NT + 4 * (g + 1) + si, 2, [("Fmm", i, 1)])
            emit_omm(NTB - 1)
            if g + 1 < NGRP and NTB < 6:
                if NTB < 4:
                    emit_ln_stats(NT + 4 * (g + 1) + si)
                    emit_ln_apply(NT + 4 * (g + 1) + si)
                emit_ln_tps(NT + 4 * (g + 1) + si, 2, [("Fmm", i, 1)])
            P.add("vector", lambda e: e.reduce_sum(
                rsm[:, :], r8[:, 0:NTB], axis=AX.X),
                waits=[("r8", i, NTB - 1)], mark=("rs", i))
            P.add("vector", lambda e: e.reciprocal(rrv[:, :], rsm[:, :]),
                  waits=[("rs", i)], mark=("rr", i))
            P.add("vector", lambda e: e.scalar_tensor_tensor(
                xr[:, :], po[:, :], rrv[:, 0:1], xr[:, :], ALU.mult, ALU.add),
                waits=[("Omm", i, NTB - 1, 1), ("rr", i), ("xrd", i)],
                mark=("cmb1", i))
            free["po"] = ("cmb1", i)
            P.add("vector", lambda e: e.tensor_add(xr[:, :], xr[:, :], pf[:, :]),
                  waits=[("Fmm", i, 0), ("Fmm", i, 1), ("cmb1", i)],
                  mark=("cmb2", i))
            free["pf"] = ("cmb2", i)
            P.add("sync", lambda e, i=i: e.dma_start(
                out_d[i * 128:(i + 1) * 128, :], xr[:, :]),
                waits=[("cmb2", i)], mark=("out", i), dma_sem="d_out")

        if g + 1 < NGRP:
            emit_q2(g + 1)

    P.add("sync", lambda e: e.nop(), waits=[("out", NT - 1)])

    with nc.Block() as block:
        P.emit(block, sems)

    es.close()
    nc.finalize()
    return nc


def kernel(L0_attn_x, L0_attn_W1, L0_attn_b1, L0_attn_W2, L0_attn_b2,
           L0_attn_W3, L0_attn_b3, L0_attn_W4, L0_attn_b4,
           L0_ffn_G1_R, L0_ffn_G2_R, L0_ffn_G1_I, L0_ffn_G2_I,
           L0_ffn_bias_real, L0_ffn_bias_imag):
    x = np.ascontiguousarray(np.asarray(L0_attn_x, dtype=np.float32))
    B, T, Dm = x.shape
    assert B == NCORES and Dm == D
    c32 = lambda a: np.ascontiguousarray(np.asarray(a, dtype=np.float32))
    shared = {
        "W1": c32(L0_attn_W1), "b1": c32(L0_attn_b1),
        "W2": c32(L0_attn_W2), "b2": c32(L0_attn_b2),
        "W3": c32(L0_attn_W3), "b3": c32(L0_attn_b3),
        "W4": c32(L0_attn_W4), "b4": c32(L0_attn_b4),
        "G1R": c32(L0_ffn_G1_R), "G2R": c32(L0_ffn_G2_R),
        "G1I": c32(L0_ffn_G1_I), "G2I": c32(L0_ffn_G2_I),
        "brl": c32(L0_ffn_bias_real), "bim": c32(L0_ffn_bias_imag),
    }
    nc = build_nc(T)
    in_maps = [dict(shared, x=np.ascontiguousarray(x[b])) for b in range(B)]
    res = run_bass_kernel_spmd(nc, in_maps, core_ids=list(range(NCORES)))
    out = np.stack([res.results[b]["out"] for b in range(B)], axis=0)
    return out.astype(np.float32)


# version 31
# speedup vs baseline: 1.2129x; 1.0083x over previous
import numpy as np
import concourse.bass as bass
import concourse.mybir as mybir
from concourse.bass_utils import run_bass_kernel_spmd
from contextlib import ExitStack

F32 = mybir.dt.float32
BF16 = mybir.dt.bfloat16
AF = mybir.ActivationFunctionType
ALU = mybir.AluOpType
AX = mybir.AxisListType

D = 1024
NCORES = 8
SCALE = 0.03125


class Prog:

    ENG_SEMS = {"sync": "q_sy", "tensor": "q_t", "vector": "q_v",
                "scalar": "q_s", "gpsimd": "q_g"}

    def __init__(self):
        self.ops = []
        self.marks = {}
        self.counts = {}

    def add(self, eng, fn, waits=(), mark=None, dma_sem=None):
        rw = []
        for w in waits:
            if w is None:
                continue
            if w not in self.marks:
                raise KeyError(f"wait on unmarked key {w!r} (op mark={mark!r})")
            rw.append(self.marks[w])
        mi = None
        if mark is not None:
            assert mark not in self.marks, f"duplicate mark {mark!r}"
            sem = dma_sem if dma_sem is not None else self.ENG_SEMS[eng]
            inc = 16 if dma_sem is not None else 1
            v = self.counts.get(sem, 0) + inc
            self.counts[sem] = v
            self.marks[mark] = (sem, v)
            mi = (sem, v, inc)
        else:
            assert dma_sem is None
        self.ops.append((eng, fn, tuple(rw), mi))

    def emit(self, block, sems):
        waited = {e: {} for e in self.ENG_SEMS}

        def make_body(engname):
            def body(eng):
                seen = waited[engname]
                for e, fn, rw, mi in self.ops:
                    if e != engname:
                        continue
                    for semname, val in rw:
                        if seen.get(semname, -1) >= val:
                            continue
                        eng.wait_ge(sems[semname], val)
                        seen[semname] = val
                    ins = fn(eng)
                    if mi is not None:
                        semname, val, inc = mi
                        ins.then_inc(sems[semname], inc)
            return body

        for engname in ("sync", "gpsimd", "vector", "tensor", "scalar"):
            getattr(block, engname)(make_body(engname))


def build_nc(T=4096):
    NT = T // 128
    NTB = T // 512
    NGRP = NT // 4
    full = (T == 4096)

    nc = bass.Bass("TRN2", target_bir_lowering=False,
                   dynamic_dma_scratch_size=8192)

    x_d = nc.dram_tensor("x", [T, D], F32, kind="ExternalInput").ap()
    w1_d = nc.dram_tensor("W1", [D, D], F32, kind="ExternalInput").ap()
    b1_d = nc.dram_tensor("b1", [D], F32, kind="ExternalInput").ap()
    w2_d = nc.dram_tensor("W2", [D, D], F32, kind="ExternalInput").ap()
    b2_d = nc.dram_tensor("b2", [D], F32, kind="ExternalInput").ap()
    w3_d = nc.dram_tensor("W3", [D, D], F32, kind="ExternalInput").ap()
    b3_d = nc.dram_tensor("b3", [D], F32, kind="ExternalInput").ap()
    w4_d = nc.dram_tensor("W4", [D, D], F32, kind="ExternalInput").ap()
    b4_d = nc.dram_tensor("b4", [D], F32, kind="ExternalInput").ap()
    g1r_d = nc.dram_tensor("G1R", [1, 32, 4, 32], F32, kind="ExternalInput").ap()
    g2r_d = nc.dram_tensor("G2R", [4, 16, 1, 16], F32, kind="ExternalInput").ap()
    g1i_d = nc.dram_tensor("G1I", [1, 32, 4, 32], F32, kind="ExternalInput").ap()
    g2i_d = nc.dram_tensor("G2I", [4, 16, 1, 16], F32, kind="ExternalInput").ap()
    br_d = nc.dram_tensor("brl", [512], F32, kind="ExternalInput").ap()
    bi_d = nc.dram_tensor("bim", [512], F32, kind="ExternalInput").ap()
    out_d = nc.dram_tensor("out", [T, D], F32, kind="ExternalOutput").ap()
    scr_d = [nc.dram_tensor(f"scr{ri}", [1024, 256], F32).ap() for ri in range(2)]

    es = ExitStack()
    E = es.enter_context

    KT = E(nc.sbuf_tensor("KT", [128, 8, T], BF16))
    V0 = E(nc.sbuf_tensor("V0", [128, NT, D], BF16))
    Wq2 = E(nc.sbuf_tensor("Wq2", [128, 4, D], BF16))
    Wproj = E(nc.sbuf_tensor("Wproj", [128, 8, D], BF16))
    Ssb = E(nc.sbuf_tensor("Ssb", [128, T], F32))
    QXT = E(nc.sbuf_tensor("QXT", [128, 8192], BF16))
    ngxT = E(nc.sbuf_tensor("ngxT", [128, 4, 512], BF16))
    xg = E(nc.sbuf_tensor("xg", [128, 2, D], F32))
    xr = E(nc.sbuf_tensor("xr", [128, D], F32))
    xnn = E(nc.sbuf_tensor("xnn", [128, 2, D], BF16))
    Et = E(nc.sbuf_tensor("Et", [128, 4, 512], BF16))
    ETs = E(nc.sbuf_tensor("ETs", [128, 4, 4, 128], BF16))
    ident = E(nc.sbuf_tensor("ident", [128, 128], BF16))
    cbs = E(nc.sbuf_tensor("cbs", [2, 1152], BF16))
    b1s = E(nc.sbuf_tensor("b1s", [128, 8], F32))
    b2s = E(nc.sbuf_tensor("b2s", [128, 8], F32))
    b3c = E(nc.sbuf_tensor("b3c", [128, 8], BF16))
    bn6 = E(nc.sbuf_tensor("bn6", [128, 2, 12], F32))
    bn2 = E(nc.sbuf_tensor("bn2", [128, 2, 2], F32))
    sd1 = E(nc.sbuf_tensor("sd1", [128, 2, 1], F32))
    vep = E(nc.sbuf_tensor("vep", [128, 2, 1], F32))
    rstd = E(nc.sbuf_tensor("rstd", [128, 2, 1], F32))
    nmr = E(nc.sbuf_tensor("nmr", [128, 2, 1], F32))
    mx8 = E(nc.sbuf_tensor("mx8", [128, 8], F32))
    mxv = E(nc.sbuf_tensor("mxv", [128, 1], F32))
    ngm = E(nc.sbuf_tensor("ngm", [128, 1], F32))
    r8 = E(nc.sbuf_tensor("r8", [128, 8], F32))
    rsm = E(nc.sbuf_tensor("rsm", [128, 1], F32))
    rrv = E(nc.sbuf_tensor("rrv", [128, 1], F32))

    cbhi = cbs[0:1, 0:1024]
    cb2 = cbs[0:2, 0:1024]
    ones1 = cbs[0:2, 1024:1152]
    cblo = None
    QT = QXT[:, 0:4096].rearrange("p (c n) -> p c n", c=8)
    xnT = QXT[:, 4096:8192].rearrange("p (u c n) -> p u c n", u=4, c=8)
    if full:
        W3T = KT[:, :, 2048:3072]
    else:
        W3T = None

    if full:
        W3B = KT[:, :, 1024:2048]
        W4b = KT[:, :, 3072:4096]
        W3p = Ssb[:, :].bitcast(BF16).rearrange("p (c n) -> p c n", c=8)
        pro_flat = V0[:, :, :].rearrange("p a n -> p (a n)")
    else:
        W3B_t = E(nc.sbuf_tensor("W3B_t", [128, 8, D], BF16))
        W4b_t = E(nc.sbuf_tensor("W4b_t", [128, 8, D], BF16))
        W3p_t = E(nc.sbuf_tensor("W3p_t", [128, 8, D], BF16))
        W3T_t = E(nc.sbuf_tensor("W3T_t", [128, 8, D], BF16))
        pro_t = E(nc.sbuf_tensor("pro_t", [128, 19456], BF16))
        W3B = W3B_t[:, :, :]
        W4b = W4b_t[:, :, :]
        W3p = W3p_t[:, :, :]
        W3T = W3T_t[:, :, :]
        pro_flat = pro_t[:, :]

    out2 = pro_flat[:, 0:4096].bitcast(F32).rearrange("p (q n) -> p q n", q=8)
    wqst = [pro_flat[:, 4096 + 1024 * k:4096 + 1024 * (k + 1)].bitcast(F32)
            for k in range(2)]
    b4row = pro_flat[0:1, 6144:8192].bitcast(F32)
    cbrow = pro_flat[0:1, 8192:10240].bitcast(F32)
    cbf = pro_flat[0:1, 10240:12288].bitcast(F32)
    g1p = pro_flat[0:4, 12288:16384].bitcast(F32).rearrange(
        "p (ri n) -> p ri n", ri=2)
    g2n = pro_flat[0:4, 16384:17408].bitcast(F32).rearrange(
        "p (ri n) -> p ri n", ri=2)
    g2p = pro_flat[0:4, 17408:18432].bitcast(F32).rearrange(
        "p (ri n) -> p ri n", ri=2)
    lo_stage = pro_flat[0:1, 18432:19456]

    sp_a = E(nc.psum_tensor("sp_a", [128, 512], F32))
    sp_b = E(nc.psum_tensor("sp_b", [128, 512], F32))
    po = E(nc.psum_tensor("po", [128, 1024], F32))
    pf = E(nc.psum_tensor("pf", [128, 1024], F32))
    ptA = E(nc.psum_tensor("ptA", [128, 4, 128], BF16))
    ptB = E(nc.psum_tensor("ptB", [128, 4, 128], BF16))

    semnames = ["q_sy", "q_t", "q_v", "q_s", "q_g",
                "d_pro", "d_wg", "d_xg0", "d_xg1", "d_xr", "d_out",
                "d_scw", "d_sra", "d_srb", "d_w10", "d_w11", "d_w12", "d_w13", "d_cbl", "d_wgB", "d_wgC", "d_wgD"]
    sems = {n: E(nc.semaphore(n)) for n in semnames}

    P = Prog()
    free = {"ptA": None, "ptB": None, "po": None, "pf": None,
            "sp_a": None, "sp_b": None}

    def proload(key, out_ap, in_ap):
        P.add("sync", lambda e, o=out_ap, i=in_ap: e.dma_start(o, i),
              mark=key, dma_sem="d_pro")

    proload(("Lg1r",), g1p[:, 0, :],
            bass.AP(g1r_d.tensor, 0, [[32, 4], [128, 32], [1, 32]]))
    proload(("Lg1i",), g1p[:, 1, :],
            bass.AP(g1i_d.tensor, 0, [[32, 4], [128, 32], [1, 32]]))
    proload(("Lg2r",), g2n[:, 0, :],
            bass.AP(g2r_d.tensor, 0, [[256, 4], [1, 256]]))
    proload(("Lg2i",), g2n[:, 1, :],
            bass.AP(g2i_d.tensor, 0, [[256, 4], [1, 256]]))
    for c in range(8):
        proload(("Lb1", c), b1s[:, c:c + 1], b1_d[c * 128:(c + 1) * 128].unsqueeze(1))
        proload(("Lb2", c), b2s[:, c:c + 1], b2_d[c * 128:(c + 1) * 128].unsqueeze(1))
    proload(("Lb4",), b4row[:, :], b4_d.unsqueeze(0))
    proload(("Lbr",), cbrow[:, 0:512], br_d.unsqueeze(0))
    proload(("Lbi",), cbrow[:, 512:1024], bi_d.unsqueeze(0))
    PRO = ("Lbi",)

    P.add("gpsimd", lambda e: e.memset(ident[:, :], 0.0), mark=("id1",))
    P.add("gpsimd", lambda e: e.affine_select(
        out=ident[:, :], in_=ident[:, :], compare_op=ALU.not_equal, fill=1.0,
        base=0, pattern=[[-1, 128]], channel_multiplier=1),
        waits=[("id1",)], mark=("id2",))
    P.add("gpsimd", lambda e: e.memset(ones1, 1.0), mark=("ones",))
    P.add("gpsimd", lambda e: e.dma_start(
        Wproj[:, :, :], w2_d.rearrange("(c p) n -> p c n", p=128)),
        mark=("LW2",), dma_sem="d_wg")
    P.add("gpsimd", lambda e: e.dma_start(
        W3B, w3_d.rearrange("(c p) n -> p c n", p=128)),
        mark=("LW3",), dma_sem="d_wgC")
    P.add("gpsimd", lambda e: e.dma_start(
        W4b, w4_d.rearrange("(c p) n -> p c n", p=128)),
        mark=("LW4",), dma_sem="d_wgB")
    P.add("gpsimd", lambda e: e.dma_start(
        b3c[:, :], bass.AP(b3_d.tensor, 0, [[1, 128], [128, 8]]),
        allow_slow_non_contiguous=True), mark=("Lb3c",), dma_sem="d_wgD")

    W3PDONE = CBDONE = WQDONE = None

    gregs = [(sp_a, 0, "sp_a"), (sp_b, 0, "sp_b"), (po, 0, "po"), (po, 512, "po")]

    def emit_g(ri):
        for q in range(8):
            spv, off, _ = gregs[q % 4]
            P.add("tensor", lambda e, ri=ri, q=q, spv=spv, off=off: e.matmul(
                spv[:, off:off + 256], g1p[:, ri, q * 128:(q + 1) * 128],
                g2p[:, ri, :], start=True, stop=True),
                waits=[PRO, ("g2perm",),
                       ("Gev", ri, q - 4) if q >= 4 else
                       (("Gev", ri - 1, q + 4) if ri > 0 else None)],
                mark=("Gmm", ri, q))
            P.add("vector", lambda e, q=q, spv=spv, off=off: e.tensor_copy(
                out2[:, q, :], spv[:, off:off + 256]),
                waits=[("Gmm", ri, q)] + ([("scw", 0)] if ri == 1 else []),
                mark=("Gev", ri, q))
        P.add("sync", lambda e, ri=ri: e.dma_start(
            scr_d[ri].rearrange("(q p) n -> p q n", p=128), out2[:, :, :]),
            waits=[("Gev", ri, 7)],
            mark=("scw", ri), dma_sem="d_scw")

    def emit_pro_tail():
        nonlocal W3PDONE, CBDONE, WQDONE
        for p4 in range(16):
            i, j4 = divmod(p4, 2)
            bank, bk = (ptA, "ptA") if p4 % 2 == 0 else (ptB, "ptB")
            j0 = 4 * j4
            for qq in range(4):
                P.add("tensor", lambda e, i=i, j0=j0, qq=qq, bank=bank:
                      e.transpose(bank[:, qq, :],
                                  W3B[:, i, (j0 + qq) * 128:(j0 + qq + 1) * 128],
                                  ident[:, :]),
                      waits=[("LW3",), ("id2",), free[bk]] if qq == 0 else [],
                      mark=("w3tp", p4) if qq == 3 else None)
            P.add("vector", lambda e, i=i, j0=j0, bank=bank: e.tensor_copy(
                W3T[:, j0:j0 + 4, i * 128:(i + 1) * 128], bank[:, :, :]),
                waits=[("w3tp", p4)], mark=("w3te", p4))
            free[bk] = ("w3te", p4)


        for ri in range(2):
            for q in range(4):
                use = ri * 4 + q
                slot = use % 2
                dsem = "d_sra" if slot == 0 else "d_srb"
                prev = ("Wqcv", (use - 2) // 4, (use - 2) % 4) if use >= 2 else None
                for dl in range(8):
                    P.add("sync", lambda e, ri=ri, q=q, dl=dl, slot=slot: e.dma_start(
                        wqst[slot][16 * dl:16 * dl + 16, :],
                        bass.AP(scr_d[ri].tensor, q * 2048 + dl * 256,
                                [[16, 16], [8192, 32], [1, 16]])),
                        waits=[("scw", ri), prev], mark=("rb", ri, q, dl),
                        dma_sem=dsem)
                P.add("vector", lambda e, ri=ri, q=q, slot=slot: e.tensor_copy(
                    Wq2[:, q, ri * 512:(ri + 1) * 512], wqst[slot][:, :]),
                    waits=[("rb", ri, q, 7)], mark=("Wqcv", ri, q))
        WQDONE = ("Wqcv", 1, 3)


        for m in range(8):
            for h in range(2):
                for k in range(8):
                    P.add("tensor", lambda e, m=m, h=h, k=k: e.matmul(
                        po[:, h * 512:(h + 1) * 512], W3T[:, k, m * 128:(m + 1) * 128],
                        W4b[:, k, h * 512:(h + 1) * 512],
                        start=(k == 0), stop=(k == 7)),
                        waits=[("w3te", 15), ("LW4",),
                               ("W3pe", m - 1, h) if m > 0 else
                               ("Gev", 1, 6 + h)],
                        mark=("W3pm", m, h) if k == 7 else None)
                P.add("scalar", lambda e, m=m, h=h: e.activation(
                    W3p[:, m, h * 512:(h + 1) * 512],
                    po[:, h * 512:(h + 1) * 512], AF.Identity),
                    waits=[("W3pm", m, h)], mark=("W3pe", m, h))
        free["po"] = ("W3pe", 7, 1)
        W3PDONE = ("W3pe", 7, 1)


        for h in range(2):
            for k in range(8):
                P.add("tensor", lambda e, h=h, k=k: e.matmul(
                    pf[0:1, h * 512:(h + 1) * 512], b3c[:, k:k + 1],
                    W4b[:, k, h * 512:(h + 1) * 512], start=(k == 0), stop=(k == 7)),
                    waits=[("LW4",), ("Lb3c",), free["pf"]],
                    mark=("cbm", h) if k == 7 else None)
        P.add("vector", lambda e: e.tensor_copy(cbf[:, :], pf[0:1, :]),
              waits=[("cbm", 0), ("cbm", 1)], mark=("cb1",))
        P.add("vector", lambda e: e.tensor_add(cbf[:, :], cbf[:, :], b4row[:, :]),
              waits=[("cb1",), PRO], mark=("cb2",))
        P.add("vector", lambda e: e.tensor_add(cbf[:, :], cbf[:, :], cbrow[:, :]),
              waits=[("cb2",)], mark=("cb3",))
        P.add("vector", lambda e: e.tensor_copy(cbhi, cbf[:, :]),
              waits=[("cb3",)], mark=("cb4",))
        P.add("vector", lambda e: e.tensor_tensor(cbf[:, :], cbf[:, :], cbhi, ALU.subtract),
              waits=[("cb4",)], mark=("cb5",))
        P.add("vector", lambda e: e.tensor_copy(lo_stage, cbf[:, :]),
              waits=[("cb5",)], mark=("cb6",))
        P.add("sync", lambda e: e.dma_start(cbs[1:2, 0:1024], lo_stage),
              waits=[("cb6",)], mark=("cb7",), dma_sem="d_cbl")
        CBDONE = ("cb7",)
        free["pf"] = ("cb1",)


    def emit_ln_load(gt):
        slot = gt % 2
        i = gt % NT
        dsem = "d_xg0" if slot == 0 else "d_xg1"
        pw = [("lnap", gt - 2), ("bns", gt - 2, 1)] if gt >= 2 else []
        P.add("scalar", lambda e, i=i, slot=slot: e.dma_start(
            xg[:, slot, :], x_d[i * 128:(i + 1) * 128, :]),
            waits=pw, mark=("xg", gt), dma_sem=dsem)

    def emit_ln_stats(gt):
        s = gt % 2
        P.add("vector", lambda e, s=s: e.bn_stats(
            bn6[:, s, 0:6], xg[:, s, 0:512]),
            waits=[("xg", gt)], mark=("bns", gt, 0))
        P.add("vector", lambda e, s=s: e.bn_stats(
            bn6[:, s, 6:12], xg[:, s, 512:1024]),
            waits=[("bns", gt, 0)], mark=("bns", gt, 1))
        P.add("vector", lambda e, s=s: e.bn_aggr(bn2[:, s, :], bn6[:, s, :]),
              waits=[("bns", gt, 1),
                     ("sqr", gt - 2) if gt >= 2 else None],
              mark=("bna", gt))
        P.add("vector", lambda e, s=s: e.tensor_scalar_add(
            vep[:, s, :], bn2[:, s, 1:2], 1e-5),
            waits=[("bna", gt), ("sqr", gt - 2) if gt >= 2 else None],
            mark=("vep", gt))

    def emit_ln_apply(gt):
        s = gt % 2
        P.add("scalar", lambda e, s=s: e.activation(
            sd1[:, s, :], vep[:, s, :], AF.Ln),
            waits=[("vep", gt), ("rstd", gt - 2) if gt >= 2 else None],
            mark=("sqr", gt))
        P.add("scalar", lambda e, s=s: e.activation(
            rstd[:, s, :], sd1[:, s, :], AF.Exp, scale=-0.5),
            waits=[("sqr", gt), ("lnap", gt - 2) if gt >= 2 else None],
            mark=("rstd", gt))
        P.add("vector", lambda e, s=s: e.tensor_scalar(
            nmr[:, s, :], bn2[:, s, 0:1], rstd[:, s, 0:1], -1.0,
            ALU.mult, ALU.mult),
            waits=[("rstd", gt)], mark=("nmr", gt))
        P.add("gpsimd", lambda e, s=s: e.tensor_scalar(
            xnn[:, s, :], xg[:, s, :], rstd[:, s, 0:1], nmr[:, s, 0:1],
            ALU.mult, ALU.add),
            waits=[("nmr", gt), ("xtp", gt - 2, 1) if gt >= 2 else None],
            mark=("lnap", gt))

    def emit_ln_chain(gt):
        emit_ln_load(gt)
        emit_ln_stats(gt)
        emit_ln_apply(gt)

    def emit_ln_tps(gt, phase, extra_tp):
        u = gt % 4
        slot = gt % 2
        for q2 in range(2):
            bank, bk = (ptA, "ptA") if q2 == 0 else (ptB, "ptB")
            c0 = 4 * q2
            xw = [("lnap", gt), free[bk]] + (list(extra_tp) if q2 == 0 else [])
            for qq in range(4):
                P.add("tensor", lambda e, c0=c0, qq=qq, slot=slot, bank=bank:
                      e.transpose(bank[:, qq, :],
                                  xnn[:, slot, (c0 + qq) * 128:(c0 + qq + 1) * 128],
                                  ident[:, :]),
                      waits=(xw + [("id2",)]) if qq == 0 else [],
                      mark=("xtp", gt, q2) if qq == 3 else None)
            P.add("vector", lambda e, u=u, c0=c0, bank=bank: e.tensor_copy(
                xnT[:, u, c0:c0 + 4, :], bank[:, :, :]),
                waits=[("xtp", gt, q2)], mark=("xte", gt, q2))
            free[bk] = ("xte", gt, q2)
        if phase == 2:
            P.add("vector", lambda e, u=u: e.tensor_scalar_mul(
                ngxT[:, u, :],
                xnT[:, u, 4:8, :].rearrange("p c n -> p (c n)"), -1.0),
                waits=[("xte", gt, 1)], mark=("ngx", gt))

    def emit_ln_tile(gt, phase, extra_tp):
        emit_ln_chain(gt)
        emit_ln_tps(gt, phase, extra_tp)

    def emit_khalf(g, half):
        off = 256 * half
        for c in range(8):
            spv, spk = (sp_a, "sp_a") if c % 2 == 0 else (sp_b, "sp_b")
            for k in range(8):
                P.add("tensor", lambda e, c=c, k=k, half=half, spv=spv, off=off:
                      e.matmul(spv[:, off:off + 256],
                               Wproj[:, k, c * 128:(c + 1) * 128],
                               xnT[:, 2 * half:2 * half + 2, k, :],
                               start=(k == 0), stop=(k == 7)),
                      waits=[("xte", 4 * g + 2 * half + 1, 1), ("LW2",),
                             free[spk]],
                      mark=("Kmm", half, g, c) if k == 7 else None)
            kw = [("Kmm", half, g, c), PRO]
            if full and g in (2, 3):
                kw.append(("w3tp", 15))
            elif full and g in (4, 5):
                kw.append(("W3pe", 7, 1))
            elif full and g in (6, 7):
                kw += [("W3pe", 7, 1), ("cbm", 1)]
            P.add("scalar", lambda e, c=c, g=g, spv=spv, off=off: e.activation(
                KT[:, c, g * 512 + off:g * 512 + off + 256],
                spv[:, off:off + 256], AF.Identity,
                bias=b2s[:, c:c + 1]), waits=kw, mark=("Kev", half, g, c))
            free[spk] = ("Kev", half, g, c)

    def emit_v(g, u):
        pv, pk = (po, "po") if u % 2 == 0 else (pf, "pf")
        for h in range(2):
            for k in range(8):
                P.add("tensor", lambda e, u=u, h=h, k=k, pv=pv: e.matmul(
                    pv[:, h * 512:(h + 1) * 512], xnT[:, u, k, :],
                    W3p[:, k, h * 512:(h + 1) * 512],
                    start=(k == 0), stop=(k == 7)),
                    waits=[("xte", 4 * g + u, 1), W3PDONE, free[pk]],
                    mark=("Vmm", g, u, h) if k == 7 else None)
        vw = [("Vmm", g, u, 0), ("Vmm", g, u, 1)]
        if full and (4 * g + u) < 19:
            vw += [WQDONE, CBDONE]
        P.add("scalar", lambda e, g=g, u=u, pv=pv: e.activation(
            V0[:, 4 * g + u, :], pv[:, :], AF.Identity),
            waits=vw, mark=("Vev", g, u))
        free[pk] = ("Vev", g, u)

    P.add("vector", lambda e: e.tensor_copy(
        g2p[:, :, :].rearrange("p ri (g e) -> p ri g e", g=16),
        g2n[:, :, :].rearrange("p ri (e g) -> p ri g e", e=16)),
        waits=[PRO], mark=("g2perm",))
    emit_g(0)
    for u in range(2):
        emit_ln_load(u)
        emit_ln_stats(u)
    emit_g(1)
    free["sp_a"] = ("Gev", 1, 4)
    free["sp_b"] = ("Gev", 1, 5)
    for u in range(2):
        emit_ln_apply(u)
    for u in range(2):
        emit_ln_tps(u, 1, [])
    for u in range(2, 4):
        emit_ln_load(u)
        emit_ln_stats(u)
    for u in range(2, 4):
        emit_ln_apply(u)
    for u in range(2, 4):
        emit_ln_tps(u, 1, [])
    for g in range(NGRP):
        nphase = 2 if g == NGRP - 1 else 1
        for u2 in range(2):
            emit_ln_load(4 * (g + 1) + u2)
            emit_ln_stats(4 * (g + 1) + u2)
        emit_khalf(g, 0)
        for u2 in range(2):
            emit_ln_apply(4 * (g + 1) + u2)
        if g == 0:
            emit_khalf(g, 1)
            emit_pro_tail()
        emit_v(g, 0)
        emit_v(g, 1)
        for u2 in range(2):
            emit_ln_tps(4 * (g + 1) + u2, nphase,
                        [("Kmm", 0, g, 7), ("Vmm", g, u2, 1)])
        for u2 in range(2, 4):
            emit_ln_load(4 * (g + 1) + u2)
            emit_ln_stats(4 * (g + 1) + u2)
        if g > 0:
            emit_khalf(g, 1)
        for u2 in range(2, 4):
            emit_ln_apply(4 * (g + 1) + u2)
        emit_v(g, 2)
        emit_v(g, 3)
        for u2 in range(2, 4):
            emit_ln_tps(4 * (g + 1) + u2, nphase,
                        [("Kmm", 1, g, 7), ("Vmm", g, u2, 1)])

    KTDONE = ("Kev", 1, NGRP - 1, 7)
    V0DONE = ("Vev", NGRP - 1, 3)

    w1B = w1_d.rearrange("(k p) (r n) -> p r k n", p=128, n=128)
    w1stage = Ssb[:, :].rearrange("p (s n) -> p s n", s=4)
    for r in range(8):
        lw = [("Kmm", 1, NGRP - 1, 7), V0DONE]
        if r >= 4:
            lw.append(("LW1", r - 4))
        P.add("sync", lambda e, r=r: e.dma_start(
            w1stage[:, r % 4, :], w1B[:, r, :, :]),
            waits=lw, mark=("LW1d", r), dma_sem="d_w1" + str(r % 4))
        P.add("vector", lambda e, r=r: e.tensor_copy(
            Wproj[:, r, :], w1stage[:, r % 4, :]),
            waits=[("LW1d", r)], mark=("LW1", r))

    def emit_q2(g):
        gt3 = NT + 4 * g + 3
        for r in range(8):
            spv, spk = (sp_a, "sp_a") if r % 2 == 0 else (sp_b, "sp_b")
            for k in range(8):
                P.add("tensor", lambda e, r=r, k=k, spv=spv: e.matmul(
                    spv[:, :], Wproj[:, r, k * 128:(k + 1) * 128],
                    xnT[:, :, k, :], start=(k == 0), stop=(k == 7)),
                    waits=[("xte", gt3, 1), ("LW1", r), free[spk]],
                    mark=("Qmm", g, r) if k == 7 else None)
            qevw = [("Qmm", g, r), PRO]
            if g >= 1:
                qevw.append(("Smm", 4 * g - 1, NTB - 1))
            P.add("scalar", lambda e, r=r, spv=spv: e.activation(
                QT[:, r, :], spv[:, :], AF.Identity, bias=b1s[:, r:r + 1]),
                waits=qevw, mark=("Qev", g, r))
            free[spk] = ("Qev", g, r)

    emit_q2(0)
    for g in range(NGRP):
        QDONE = ("Qev", g, 7)
        for si in range(4):
            i = 4 * g + si
            gt = NT + i
            P.add("sync", lambda e, i=i: e.dma_start(
                xr[:, :], x_d[i * 128:(i + 1) * 128, :]),
                waits=[("out", i - 1)] if i >= 1 else [], mark=("xrd", i),
                dma_sem="d_xr")
            if g + 1 < NGRP:
                emit_ln_load(NT + 4 * (g + 1) + si)
            free["sp_c"] = free["po"]
            free["sp_d"] = free["po"]
            sslots = [(sp_a[:, :], "sp_a"), (sp_b[:, :], "sp_b"),
                      (po[:, 0:512], "sp_c"), (po[:, 512:1024], "sp_d")]
            for tb in range(NTB):
                spv, spk = sslots[tb % 4]
                sw = [QDONE, KTDONE, free[spk]]
                for c in range(8):
                    P.add("tensor", lambda e, tb=tb, c=c, si=si, spv=spv: e.matmul(
                        spv, QT[:, c, si * 128:(si + 1) * 128],
                        KT[:, c, tb * 512:(tb + 1) * 512],
                        start=(c == 0), stop=(c == 7)),
                        waits=sw, mark=("Smm", i, tb) if c == 7 else None)
                scw_ = [("Smm", i, tb)]
                if i >= 1:
                    scw_.append(("exp", i - 1, tb))
                else:
                    scw_.append(("LW1", 7))
                P.add("scalar", lambda e, tb=tb, spv=spv: e.activation(
                    Ssb[:, tb * 512:(tb + 1) * 512], spv, AF.Identity),
                    waits=scw_, mark=("Scp", i, tb))
                P.add("vector", lambda e, tb=tb, spv=spv: e.reduce_max(
                    mx8[:, tb:tb + 1], spv, axis=AX.X),
                    waits=[("Smm", i, tb), ("Scp", i, tb)],
                    mark=("Smx", i, tb))
                free[spk] = ("Smx", i, tb)
            P.add("vector", lambda e: e.reduce_max(
                mxv[:, :], mx8[:, 0:NTB], axis=AX.X),
                waits=[("Smx", i, NTB - 1)], mark=("mxc", i))
            P.add("vector", lambda e: e.tensor_scalar_mul(
                ngm[:, :], mxv[:, :], -SCALE),
                waits=[("mxc", i)] +
                      ([("exp", i - 1, NTB - 1)] if i >= 1 else []),
                mark=("ngm", i))
            for o in range(2):
                fw = [("xte", gt, 1), WQDONE, CBDONE, ("ones",), free["pf"],
                      ("ngx", gt)]
                reg = pf[:, o * 512:(o + 1) * 512]
                for c in range(4):
                    P.add("tensor", lambda e, o=o, c=c, si=si, reg=reg: e.matmul(
                        reg, xnT[:, si, c, :], Wq2[:, c, o * 512:(o + 1) * 512],
                        start=(c == 0), stop=False), waits=fw, mark=None)
                for cc in range(4):
                    if o == 0:
                        P.add("tensor", lambda e, cc=cc, si=si, reg=reg: e.matmul(
                            reg, ngxT[:, si, cc * 128:(cc + 1) * 128],
                            Wq2[:, cc, 512:1024], start=False, stop=False),
                            mark=None)
                    else:
                        P.add("tensor", lambda e, cc=cc, si=si, reg=reg: e.matmul(
                            reg, xnT[:, si, 4 + cc, :], Wq2[:, cc, 0:512],
                            start=False, stop=False), mark=None)
                P.add("tensor", lambda e, o=o, reg=reg: e.matmul(
                    reg, ones1, cb2[:, o * 512:(o + 1) * 512],
                    start=False, stop=True), mark=("Fmm", i, o))
            def emit_exp(tb):
                ew = [("ngm", i), ("Scp", i, tb)]
                if tb >= 4:
                    pu = (i, tb - 4)
                elif i >= 1:
                    pu = (i - 1, NTB - 4 + tb) if NTB >= 4 else (i - 1, 0)
                else:
                    pu = None
                if pu is not None and pu[1] >= 0:
                    ew += [("r8",) + pu, ("etp",) + pu + (1,)]
                P.add("scalar", lambda e, tb=tb: e.activation(
                    Et[:, tb % 4, :], Ssb[:, tb * 512:(tb + 1) * 512],
                    AF.Exp, bias=ngm[:, 0:1], scale=SCALE),
                    waits=ew, mark=("exp", i, tb))
                P.add("vector", lambda e, tb=tb: e.reduce_sum(
                    r8[:, tb:tb + 1], Et[:, tb % 4, :], axis=AX.X),
                    waits=[("exp", i, tb)] +
                          ([("rr", i - 1)] if (tb == 0 and i >= 1) else []),
                    mark=("r8", i, tb))

            def emit_tp(tb):
                bank, bk = (ptA, "ptA") if tb % 2 == 0 else (ptB, "ptB")
                for q in range(4):
                    P.add("tensor", lambda e, tb=tb, q=q, bank=bank:
                          e.transpose(bank[:, q, :],
                                      Et[:, tb % 4, q * 128:(q + 1) * 128],
                                      ident[:, :]),
                          waits=[("exp", i, tb), free[bk]] if q == 0 else [],
                          mark=("etp", i, tb, 1) if q == 3 else None)
                if tb >= 4:
                    slotprev = ("Omm", i, tb - 4, 1)
                elif i >= 1:
                    slotprev = ("Omm", i - 1,
                                NTB - 4 + tb if NTB >= 4 else 0, 1)
                else:
                    slotprev = None
                P.add("vector", lambda e, tb=tb, bank=bank: e.tensor_copy(
                    ETs[:, tb % 4, :, :], bank[:, :, :]),
                    waits=[("etp", i, tb, 1), slotprev],
                    mark=("ete", i, tb, 1))
                free[bk] = ("ete", i, tb, 1)

            def emit_omm(tb):
                ow = [("ete", i, tb, 1), V0DONE]
                if tb == 0:
                    ow += [free["po"], ("cmb1", i - 1) if i >= 1 else None]
                    if NTB >= 3:
                        ow += [("Scp", i, NTB - 2), ("Scp", i, NTB - 1)]
                    else:
                        ow += [("Scp", i, tbq) for tbq in range(NTB)
                               if tbq % 4 >= 2]
                for q in range(4):
                    tj = 4 * tb + q
                    P.add("tensor", lambda e, tb=tb, q=q, tj=tj: e.matmul(
                        po[:, 0:512], ETs[:, tb % 4, q, :], V0[:, tj, 0:512],
                        start=(tj == 0), stop=(tj == NT - 1)),
                        waits=ow if q == 0 else [], mark=None)
                    P.add("tensor", lambda e, tb=tb, q=q, tj=tj: e.matmul(
                        po[:, 512:1024], ETs[:, tb % 4, q, :], V0[:, tj, 512:1024],
                        start=(tj == 0), stop=(tj == NT - 1)),
                        mark=("Omm", i, tb, 1) if q == 3 else None)

            emit_exp(0)
            emit_tp(0)
            for tb in range(1, NTB):
                emit_exp(tb)
                emit_tp(tb)
                emit_omm(tb - 1)
                if g + 1 < NGRP:
                    if tb == 1:
                        emit_ln_stats(NT + 4 * (g + 1) + si)
                    elif tb == 3:
                        emit_ln_apply(NT + 4 * (g + 1) + si)
                    elif tb == 5:
                        emit_ln_tps(NT + 4 * (g + 1) + si, 2, [("Fmm", i, 1)])
            emit_omm(NTB - 1)
            if g + 1 < NGRP and NTB < 6:
                if NTB < 4:
                    emit_ln_stats(NT + 4 * (g + 1) + si)
                    emit_ln_apply(NT + 4 * (g + 1) + si)
                emit_ln_tps(NT + 4 * (g + 1) + si, 2, [("Fmm", i, 1)])
            P.add("vector", lambda e: e.reduce_sum(
                rsm[:, :], r8[:, 0:NTB], axis=AX.X),
                waits=[("r8", i, NTB - 1)], mark=("rs", i))
            P.add("vector", lambda e: e.reciprocal(rrv[:, :], rsm[:, :]),
                  waits=[("rs", i)], mark=("rr", i))
            P.add("vector", lambda e: e.scalar_tensor_tensor(
                xr[:, :], po[:, :], rrv[:, 0:1], xr[:, :], ALU.mult, ALU.add),
                waits=[("Omm", i, NTB - 1, 1), ("rr", i), ("xrd", i)],
                mark=("cmb1", i))
            free["po"] = ("cmb1", i)
            P.add("vector", lambda e: e.tensor_add(xr[:, :], xr[:, :], pf[:, :]),
                  waits=[("Fmm", i, 0), ("Fmm", i, 1), ("cmb1", i)],
                  mark=("cmb2", i))
            free["pf"] = ("cmb2", i)
            P.add("sync", lambda e, i=i: e.dma_start(
                out_d[i * 128:(i + 1) * 128, :], xr[:, :]),
                waits=[("cmb2", i)], mark=("out", i), dma_sem="d_out")

        if g + 1 < NGRP:
            emit_q2(g + 1)

    P.add("sync", lambda e: e.nop(), waits=[("out", NT - 1)])

    with nc.Block() as block:
        P.emit(block, sems)

    es.close()
    nc.finalize()
    return nc


def kernel(L0_attn_x, L0_attn_W1, L0_attn_b1, L0_attn_W2, L0_attn_b2,
           L0_attn_W3, L0_attn_b3, L0_attn_W4, L0_attn_b4,
           L0_ffn_G1_R, L0_ffn_G2_R, L0_ffn_G1_I, L0_ffn_G2_I,
           L0_ffn_bias_real, L0_ffn_bias_imag):
    x = np.ascontiguousarray(np.asarray(L0_attn_x, dtype=np.float32))
    B, T, Dm = x.shape
    assert B == NCORES and Dm == D
    c32 = lambda a: np.ascontiguousarray(np.asarray(a, dtype=np.float32))
    shared = {
        "W1": c32(L0_attn_W1), "b1": c32(L0_attn_b1),
        "W2": c32(L0_attn_W2), "b2": c32(L0_attn_b2),
        "W3": c32(L0_attn_W3), "b3": c32(L0_attn_b3),
        "W4": c32(L0_attn_W4), "b4": c32(L0_attn_b4),
        "G1R": c32(L0_ffn_G1_R), "G2R": c32(L0_ffn_G2_R),
        "G1I": c32(L0_ffn_G1_I), "G2I": c32(L0_ffn_G2_I),
        "brl": c32(L0_ffn_bias_real), "bim": c32(L0_ffn_bias_imag),
    }
    nc = build_nc(T)
    in_maps = [dict(shared, x=np.ascontiguousarray(x[b])) for b in range(B)]
    res = run_bass_kernel_spmd(nc, in_maps, core_ids=list(range(NCORES)))
    out = np.stack([res.results[b]["out"] for b in range(B)], axis=0)
    return out.astype(np.float32)
